# version 05 (final)
# baseline (speedup 1.0000x reference)
import numpy as np
import ml_dtypes
from contextlib import ExitStack

import concourse.bass as bass
import concourse.mybir as mybir
from concourse.bass_utils import run_bass_kernel_spmd

F32 = mybir.dt.float32
BF16 = mybir.dt.bfloat16
AF = mybir.ActivationFunctionType
ALU = mybir.AluOpType
AX = mybir.AxisListType
BF = ml_dtypes.bfloat16

D = 2048
TOK = 1024
NCORE = 8
G = D // 128
H0 = 4096
H1 = 5632
NST = 16
SEQ = 2048


class Sem:
    def __init__(self, stack, nc, name):
        self.h = stack.enter_context(nc.semaphore(name))
        self.v = 0


class PP:

    def __init__(self, stack, nc, name):
        self.s = [Sem(stack, nc, name + "A"), Sem(stack, nc, name + "B")]

    def issue(self, i):
        s = self.s[i % 2]
        s.v += 16
        return s.h

    def wait(self, eng, i):
        s = self.s[i % 2]
        eng.wait_ge(s.h, 16 * (i // 2 + 1) + s.base if hasattr(s, 'base')
                    else 16 * (i // 2 + 1))


class PPStream:

    def __init__(self, pp):
        self.pp = pp
        self.base = [pp.s[0].v, pp.s[1].v]

    def inc(self, i):
        return self.pp.issue(i)

    def wait(self, eng, i):
        s = self.pp.s[i % 2]
        eng.wait_ge(s.h, self.base[i % 2] + 16 * (i // 2 + 1))


def f32ap(nc, name, shape, kind="ExternalInput"):
    return nc.dram_tensor(name, list(shape), F32, kind=kind).ap()


def bf16ap(nc, name, shape, kind="ExternalInput"):
    return nc.dram_tensor(name, list(shape), BF16, kind=kind).ap()


def wslab_ap(w_ap, n_kg, mg, mg_size=128):
    r = w_ap.rearrange("(kg p) m -> p kg m", p=128)
    return r[:, 0:n_kg, mg * mg_size:(mg + 1) * mg_size]



def emit_ln_stats_finish(nc, sems, s1_ps, s2_ps, n_tc, stat_row, mb_ps, rb_ps,
                         ones_r, junk=None):
    sv, ss = sems['v'], sems['s']
    W = n_tc * 512
    mrow = stat_row[0:1, 0:W]
    vrow = stat_row[0:1, W:2 * W]
    trow = stat_row[0:1, 2 * W:3 * W]
    srow = stat_row[0:1, 3 * W:4 * W]
    with nc.Block() as blk:
        @blk.vector
        def _(v):
            for tc in range(n_tc):
                v.tensor_scalar_mul(mrow[0:1, tc * 512:(tc + 1) * 512],
                                    s1_ps[tc], 1.0 / D)
            for tc in range(n_tc):
                v.tensor_scalar_mul(vrow[0:1, tc * 512:(tc + 1) * 512],
                                    s2_ps[tc], 1.0 / D)
            v.tensor_mul(trow, mrow, mrow)
            v.tensor_sub(vrow, vrow, trow)
            v.tensor_scalar_add(vrow, vrow, 1e-5).then_inc(sv.h, 1)
        sv.v += 1
        vmark = sv.v

        @blk.scalar
        def _(s):
            s.wait_ge(sv.h, vmark)
            s.activation(srow, vrow, AF.Sqrt).then_inc(ss.h, 1)
        ss.v += 1
        smark = ss.v

        @blk.vector
        def _(v):
            v.wait_ge(ss.h, smark)
            v.reciprocal(vrow, srow)
            if junk is not None:
                v.tensor_copy(junk, junk)
                v.tensor_copy(junk, junk)
    with nc.Block() as blk:
        @blk.tensor
        def _(t):
            for tc in range(n_tc):
                t.matmul(mb_ps[tc], ones_r,
                         mrow[0:1, tc * 512:(tc + 1) * 512],
                         start=True, stop=True)
            for tc in range(n_tc):
                t.matmul(rb_ps[tc], ones_r,
                         vrow[0:1, tc * 512:(tc + 1) * 512],
                         start=True, stop=True)


def emit_ln_stream(nc, sems, pp_ld, pp_st, src_hbm, dst_hbm, ttok, ones_c,
                   ones_r, stat_row, wait_sems=()):
    n_tc = ttok // 512
    assert n_tc == 2
    src3 = src_hbm.rearrange("(g p) t -> g p t", p=128)
    dst3 = dst_hbm.rearrange("(g p) t -> g p t", p=128)
    with nc.sbuf_tensor("lnx0", [128, ttok], F32) as lnx0, \
         nc.sbuf_tensor("lnx1", [128, ttok], F32) as lnx1, \
         nc.sbuf_tensor("lnsq", [128, ttok], F32) as lnsq, \
         nc.sbuf_tensor("lnu0", [128, ttok], BF16) as lnu0, \
         nc.sbuf_tensor("lnu1", [128, ttok], BF16) as lnu1, \
         nc.psum_tensor("ls1a", [1, 512], F32) as ls1a, \
         nc.psum_tensor("ls1b", [1, 512], F32) as ls1b, \
         nc.psum_tensor("lmba", [128, 512], F32) as lmba, \
         nc.psum_tensor("lmbb", [128, 512], F32) as lmbb, \
         nc.psum_tensor("lrba", [128, 512], F32) as lrba, \
         nc.psum_tensor("lrbb", [128, 512], F32) as lrbb:
        xs = [lnx0, lnx1]
        us = [lnu0, lnu1]
        s1_ps = [ls1a[:], ls1b[:]]
        mb_ps = [lmba[:], lmbb[:]]
        rb_ps = [lrba[:], lrbb[:]]
        W = n_tc * 512
        mrow = stat_row[0:1, 0:W]
        vrow = stat_row[0:1, W:2 * W]
        srow = stat_row[0:1, 3 * W:4 * W]
        sv, ss = sems['v'], sems['s']

        mm0 = sems['mm'].v
        ld = PPStream(pp_ld)
        with nc.Block() as blk:
            @blk.sync
            def _(e):
                for hs, hv in wait_sems:
                    e.wait_ge(hs, hv)
                for g in range(G):
                    if g >= 2:
                        e.wait_ge(sems['mm'].h, mm0 + g - 1)
                    e.dma_start(out=xs[g % 2][:, :],
                                in_=src3[g, :, :]).then_inc(ld.inc(g), 16)

            @blk.tensor
            def _(t):
                for g in range(G):
                    ld.wait(t, g)
                    for tc in range(2):
                        mmi = t.matmul(s1_ps[tc], ones_c,
                                       xs[g % 2][:, tc * 512:(tc + 1) * 512],
                                       start=(g == 0), stop=(g == G - 1))
                    mmi.then_inc(sems['mm'].h, 1)
            sems['mm'].v += G

        with nc.Block() as blk:
            @blk.vector
            def _(v):
                for tc in range(n_tc):
                    v.tensor_scalar_mul(mrow[0:1, tc * 512:(tc + 1) * 512],
                                        s1_ps[tc], 1.0 / D)
                v.tensor_copy(lnsq[:, 0:256], xs[0][:, 0:256])
                v.tensor_copy(lnsq[:, 0:256], xs[1][:, 0:256])
        with nc.Block() as blk:
            @blk.tensor
            def _(t):
                for tc in range(n_tc):
                    t.matmul(mb_ps[tc], ones_r,
                             mrow[0:1, tc * 512:(tc + 1) * 512],
                             start=True, stop=True)

        mm0 = sems['mm'].v
        sq0 = sems['sq'].v
        v0 = sems['v'].v
        ld = PPStream(pp_ld)
        with nc.Block() as blk:
            @blk.sync
            def _(e):
                for g in range(G):
                    if g >= 2:
                        e.wait_ge(sems['mm'].h, mm0 + g - 1)
                    e.dma_start(out=xs[g % 2][:, :],
                                in_=src3[g, :, :]).then_inc(ld.inc(g), 16)

            @blk.vector
            def _(v):
                for g in range(G):
                    ld.wait(v, g)
                    if g >= 1:
                        v.wait_ge(sems['sq'].h, sq0 + g)
                    for tc in range(2):
                        vi = v.scalar_tensor_tensor(
                            xs[g % 2][:, tc * 512:(tc + 1) * 512],
                            xs[g % 2][:, tc * 512:(tc + 1) * 512], 1.0,
                            mb_ps[tc], op0=ALU.bypass, op1=ALU.subtract)
                    vi.then_inc(sems['v'].h, 1)
            sems['v'].v += G

            @blk.scalar
            def _(s):
                for g in range(G):
                    s.wait_ge(sems['v'].h, v0 + g + 1)
                    if g >= 1:
                        s.wait_ge(sems['mm'].h, mm0 + g)
                    s.activation(lnsq[:, :], xs[g % 2][:, :],
                                 AF.Square).then_inc(sems['sq'].h, 1)
            sems['sq'].v += G

            @blk.tensor
            def _(t):
                for g in range(G):
                    t.wait_ge(sems['sq'].h, sq0 + g + 1)
                    for tc in range(2):
                        mmi = t.matmul(s1_ps[tc], ones_c,
                                       lnsq[:, tc * 512:(tc + 1) * 512],
                                       start=(g == 0), stop=(g == G - 1))
                    mmi.then_inc(sems['mm'].h, 1)
            sems['mm'].v += G

        with nc.Block() as blk:
            @blk.vector
            def _(v):
                for tc in range(n_tc):
                    v.tensor_scalar(vrow[0:1, tc * 512:(tc + 1) * 512],
                                    s1_ps[tc], 1.0 / D, 1e-5,
                                    op0=ALU.mult, op1=ALU.add) \
                     .then_inc(sv.h, 1)
            sv.v += n_tc
            vmark = sv.v

            @blk.scalar
            def _(s):
                s.wait_ge(sv.h, vmark)
                s.activation(srow, vrow, AF.Sqrt).then_inc(ss.h, 1)
            ss.v += 1
            smark = ss.v

            @blk.vector
            def _(v):
                v.wait_ge(ss.h, smark)
                v.reciprocal(vrow, srow)
                v.tensor_copy(lnsq[:, 0:256], xs[0][:, 0:256])
                v.tensor_copy(lnsq[:, 0:256], xs[1][:, 0:256])
        with nc.Block() as blk:
            @blk.tensor
            def _(t):
                for tc in range(n_tc):
                    t.matmul(rb_ps[tc], ones_r,
                             vrow[0:1, tc * 512:(tc + 1) * 512],
                             start=True, stop=True)

        v0 = sems['v'].v
        ld = PPStream(pp_ld)
        stm = PPStream(pp_st)
        with nc.Block() as blk:
            @blk.sync
            def _(e):
                for g in range(G):
                    if g >= 2:
                        e.wait_ge(sems['v'].h, v0 + 2 * (g - 1))
                    e.dma_start(out=xs[g % 2][:, :],
                                in_=src3[g, :, :]).then_inc(ld.inc(g), 16)

            @blk.vector
            def _(v):
                for g in range(G):
                    ld.wait(v, g)
                    if g >= 2:
                        stm.wait(v, g - 2)
                    for tc in range(2):
                        v.scalar_tensor_tensor(
                            lnsq[:, tc * 512:(tc + 1) * 512],
                            xs[g % 2][:, tc * 512:(tc + 1) * 512], 1.0,
                            mb_ps[tc], op0=ALU.bypass, op1=ALU.subtract)
                        tm = v.tensor_mul(
                            us[g % 2][:, tc * 512:(tc + 1) * 512],
                            lnsq[:, tc * 512:(tc + 1) * 512], rb_ps[tc])
                        if tc == 1:
                            tm.then_inc(sems['v'].h, 2)
            sems['v'].v += 2 * G

            @blk.gpsimd
            def _(g2):
                for g in range(G):
                    g2.wait_ge(sems['v'].h, v0 + 2 * (g + 1))
                    g2.dma_start(out=dst3[g, :, :],
                                 in_=us[g % 2][:, :]).then_inc(stm.inc(g), 16)



def build_ffn0():
    nc = bass.Bass("TRN2", target_bir_lowering=False,
                   detect_race_conditions=False)
    xin = f32ap(nc, "x", [D, TOK])
    addin = bf16ap(nc, "add", [D, TOK])
    W1T = f32ap(nc, "w1t", [D, H0])
    W2T = f32ap(nc, "w2t", [H0, H0])
    W3T = f32ap(nc, "w3t", [H0, D])
    b1r = f32ap(nc, "b1r", [128, H0 // 128])
    b2r = f32ap(nc, "b2r", [128, H0 // 128])
    b3r = f32ap(nc, "b3r", [128, G])
    xo = f32ap(nc, "xo", [D, TOK], kind="ExternalOutput")
    uo = bf16ap(nc, "uo", [D, TOK], kind="ExternalOutput")
    xmid = f32ap(nc, "xmid", [D, TOK], kind="Internal")

    NB1 = H0 // 128

    st = ExitStack()
    with st:
        E = st.enter_context
        sems = {k: Sem(st, nc, "sem_" + k) for k in
                ["ldx", "ldb", "mm", "ev", "sq", "y3", "spill", "v", "s"]}
        pp_w = PP(st, nc, "ppw")
        pp_g = PP(st, nc, "ppg")
        pp_s = PP(st, nc, "pps")
        pp_o = PP(st, nc, "ppo")
        ones_c = E(nc.sbuf_tensor("ones_c", [128, 1], F32))
        ones_r = E(nc.sbuf_tensor("ones_r", [1, 128], F32))
        biases = E(nc.sbuf_tensor("biases", [128, 2 * NB1 + G], F32))
        stat_row = E(nc.sbuf_tensor("stat_row", [1, 8 * 512], F32))

        es_h1 = ExitStack()
        h1 = es_h1.enter_context(nc.sbuf_tensor("h1", [128, NB1 * TOK], BF16))
        es_xln = ExitStack()
        xln = es_xln.enter_context(
            nc.sbuf_tensor("xln", [128, G * TOK], BF16))

        es_a = ExitStack()
        x_sb = es_a.enter_context(nc.sbuf_tensor("x_sb", [128, G * TOK], F32))
        ad0 = es_a.enter_context(nc.sbuf_tensor("ad0", [128, TOK], BF16))
        ad1 = es_a.enter_context(nc.sbuf_tensor("ad1", [128, TOK], BF16))
        xsq = es_a.enter_context(nc.sbuf_tensor("xsq", [128, TOK], F32))
        s1_ps = [es_a.enter_context(nc.psum_tensor(f"s1{i}", [1, 512], F32))[:]
                 for i in range(2)]
        s2_ps = [es_a.enter_context(nc.psum_tensor(f"s2{i}", [1, 512], F32))[:]
                 for i in range(2)]
        mb_ps = [es_a.enter_context(
            nc.psum_tensor(f"mb{i}", [128, 512], F32))[:] for i in range(2)]
        rb_ps = [es_a.enter_context(
            nc.psum_tensor(f"rb{i}", [128, 512], F32))[:] for i in range(2)]
        ads = [ad0, ad1]

        x3 = xin.rearrange("(g p) t -> p g t", p=128)
        a3 = addin.rearrange("(g p) t -> g p t", p=128)
        v0 = sems['v'].v
        sq0 = sems['sq'].v
        mm0 = sems['mm'].v
        ld_ad = PPStream(pp_g)
        with nc.Block() as blk:
            @blk.sync
            def _(e):
                e.dma_start(out=x_sb[:].rearrange("p (g t) -> p g t", t=TOK),
                            in_=x3).then_inc(sems['ldx'].h, 16)
            sems['ldx'].v += 16

            @blk.gpsimd
            def _(g2):
                g2.memset(ones_c[:, :], 1.0)
                g2.memset(ones_r[:, :], 1.0)
                g2.dma_start(out=biases[:, 0:NB1],
                             in_=b1r).then_inc(sems['ldb'].h, 16)
                g2.dma_start(out=biases[:, NB1:2 * NB1],
                             in_=b2r).then_inc(sems['ldb'].h, 16)
                g2.dma_start(out=biases[:, 2 * NB1:],
                             in_=b3r).then_inc(sems['ldb'].h, 16)
                for g in range(G):
                    if g >= 2:
                        g2.wait_ge(sems['v'].h, v0 + g - 1)
                    g2.dma_start(out=ads[g % 2][:, :],
                                 in_=a3[g, :, :]).then_inc(ld_ad.inc(g), 16)
            sems['ldb'].v += 48

            @blk.vector
            def _(v):
                v.wait_ge(sems['ldx'].h, 16)
                for g in range(G):
                    ld_ad.wait(v, g)
                    v.tensor_add(x_sb[:, g * TOK:(g + 1) * TOK],
                                 x_sb[:, g * TOK:(g + 1) * TOK],
                                 ads[g % 2][:, :]) \
                     .then_inc(sems['v'].h, 1)
            sems['v'].v += G

            @blk.scalar
            def _(s):
                for g in range(G):
                    s.wait_ge(sems['v'].h, v0 + g + 1)
                    if g >= 1:
                        s.wait_ge(sems['mm'].h, mm0 + g)
                    s.activation(xsq[:, :], x_sb[:, g * TOK:(g + 1) * TOK],
                                 AF.Square).then_inc(sems['sq'].h, 1)
            sems['sq'].v += G

            @blk.tensor
            def _(t):
                for g in range(G):
                    t.wait_ge(sems['sq'].h, sq0 + g + 1)
                    for tc in range(2):
                        t.matmul(s1_ps[tc], ones_c[:, :],
                                 x_sb[:, g * TOK + tc * 512:
                                      g * TOK + (tc + 1) * 512],
                                 start=(g == 0), stop=(g == G - 1))
                        mmi = t.matmul(s2_ps[tc], ones_c[:, :],
                                       xsq[:, tc * 512:(tc + 1) * 512],
                                       start=(g == 0), stop=(g == G - 1))
                    mmi.then_inc(sems['mm'].h, 1)
            sems['mm'].v += G

        with nc.Block() as blk:
            @blk.sync
            def _(e):
                e.dma_start(out=xmid.rearrange("(g p) t -> p g t", p=128),
                            in_=x_sb[:].rearrange("p (g t) -> p g t", t=TOK)) \
                 .then_inc(sems['spill'].h, 16)
            sems['spill'].v += 16

        emit_ln_stats_finish(nc, sems, s1_ps, s2_ps, 2, stat_row,
                             mb_ps, rb_ps, ones_r[:, :],
                             junk=xsq[:, 0:256])

        with nc.Block() as blk:
            @blk.vector
            def _(v):
                for g in range(G):
                    for tc in range(2):
                        sl = slice(g * TOK + tc * 512, g * TOK + (tc + 1) * 512)
                        v.scalar_tensor_tensor(
                            xsq[:, tc * 512:(tc + 1) * 512], x_sb[:, sl], 1.0,
                            mb_ps[tc], op0=ALU.bypass, op1=ALU.subtract)
                        v.tensor_mul(xln[:, sl],
                                     xsq[:, tc * 512:(tc + 1) * 512],
                                     rb_ps[tc])

        es_a.close()

        with nc.sbuf_tensor("wbufA", [128, G * 128], BF16) as wbufA, \
             nc.sbuf_tensor("wbufB", [128, G * 128], BF16) as wbufB, \
             nc.psum_tensor("pA", [128, 512], F32) as pA, \
             nc.psum_tensor("pB", [128, 512], F32) as pB:
            wbufs = [wbufA, wbufB]
            psums = [pA[:], pB[:]]
            mm0 = sems['mm'].v
            ev0 = sems['ev'].v
            ldb_all = sems['ldb'].v
            ld = PPStream(pp_w)
            with nc.Block() as blk:
                @blk.gpsimd
                def _(g2):
                    for mg in range(NB1):
                        if mg >= 2:
                            g2.wait_ge(sems['mm'].h, mm0 + 2 * (mg - 1))
                        g2.dma_start(
                            out=wbufs[mg % 2][:].rearrange(
                                "p (kg m) -> p kg m", m=128),
                            in_=wslab_ap(W1T, G, mg)).then_inc(ld.inc(mg), 16)

                @blk.tensor
                def _(t):
                    for mg in range(NB1):
                        ld.wait(t, mg)
                        for nb in range(2):
                            if mg >= 1:
                                t.wait_ge(sems['ev'].h,
                                          ev0 + 2 * (mg - 1) + nb + 1)
                            for kg in range(G):
                                mmi = t.matmul(
                                    psums[nb],
                                    wbufs[mg % 2][:, kg * 128:(kg + 1) * 128],
                                    xln[:, kg * TOK + nb * 512:
                                        kg * TOK + (nb + 1) * 512],
                                    start=(kg == 0), stop=(kg == G - 1))
                            mmi.then_inc(sems['mm'].h, 1)
                sems['mm'].v += 2 * NB1

                @blk.scalar
                def _(s):
                    s.wait_ge(sems['ldb'].h, ldb_all)
                    for mg in range(NB1):
                        for nb in range(2):
                            s.wait_ge(sems['mm'].h, mm0 + 2 * mg + nb + 1)
                            s.activation(
                                h1[:, mg * TOK + nb * 512:
                                   mg * TOK + (nb + 1) * 512],
                                psums[nb], AF.Relu,
                                bias=biases[:, mg:mg + 1]) \
                             .then_inc(sems['ev'].h, 1)
                sems['ev'].v += 2 * NB1

        es_xln.close()

        es_y3 = ExitStack()
        y3acc = es_y3.enter_context(
            nc.sbuf_tensor("y3acc", [128, G * TOK], F32))
        with nc.sbuf_tensor("w2sA", [128, NB1 * 256], BF16) as w2sA, \
             nc.sbuf_tensor("w2sB", [128, NB1 * 256], BF16) as w2sB, \
             nc.sbuf_tensor("w3sA", [128, 2 * D], BF16) as w3sA, \
             nc.sbuf_tensor("w3sB", [128, 2 * D], BF16) as w3sB, \
             nc.sbuf_tensor("h2blk", [128, 2 * TOK], BF16) as h2blk, \
             nc.psum_tensor("ph2a", [128, 512], F32) as ph2a, \
             nc.psum_tensor("ph2b", [128, 512], F32) as ph2b, \
             nc.psum_tensor("poa", [128, 512], F32) as poa, \
             nc.psum_tensor("pob", [128, 512], F32) as pob:
            w2s = [w2sA, w2sB]
            w3s = [w3sA, w3sB]
            ph2 = [ph2a[:], ph2b[:]]
            po = [poa[:], pob[:]]
            NHB = H0 // 256
            NKG2 = H0 // 128
            mm0 = sems['mm'].v
            ev0 = sems['ev'].v
            y30 = sems['y3'].v
            ld = PPStream(pp_w)
            ld3 = PPStream(pp_g)
            with nc.Block() as blk:
                @blk.gpsimd
                def _(g2):
                    g2.memset(y3acc[:, :], 0.0).then_inc(sems['y3'].h, 1)
                    for hb in range(NHB):
                        if hb >= 2:
                            g2.wait_ge(sems['mm'].h, mm0 + 36 * (hb - 1))
                        w2src = W2T.rearrange("(kg p) m -> p kg m", p=128)[
                            :, :, hb * 256:(hb + 1) * 256]
                        g2.dma_start(
                            out=w2s[hb % 2][:].rearrange(
                                "p (kg m) -> p kg m", m=256),
                            in_=w2src).then_inc(ld.inc(hb), 16)
                        w3src = W3T.rearrange("(kg p) m -> p kg m", p=128)[
                            :, 2 * hb:2 * hb + 2, :]
                        g2.dma_start(
                            out=w3s[hb % 2][:].rearrange(
                                "p (kg m) -> p kg m", m=D),
                            in_=w3src).then_inc(ld3.inc(hb), 16)
                sems['y3'].v += 1
                y3base = y30 + 1

                @blk.tensor
                def _(t):
                    for hb in range(NHB):
                        ld.wait(t, hb)
                        ld3.wait(t, hb)
                        for sub in range(2):
                            for nb in range(2):
                                gi = 2 * sub + nb
                                if hb >= 1 or gi >= 2:
                                    t.wait_ge(sems['ev'].h,
                                              ev0 + 4 * hb + gi - 1)
                                for kg in range(NKG2):
                                    mmi = t.matmul(
                                        ph2[gi % 2],
                                        w2s[hb % 2][:, kg * 256 + sub * 128:
                                            kg * 256 + sub * 128 + 128],
                                        h1[:, kg * TOK + nb * 512:
                                           kg * TOK + (nb + 1) * 512],
                                        start=(kg == 0), stop=(kg == NKG2 - 1))
                                mmi.then_inc(sems['mm'].h, 1)
                        t.wait_ge(sems['ev'].h, ev0 + 4 * (hb + 1))
                        for dg in range(G):
                            for nb in range(2):
                                gi = 2 * dg + nb
                                if hb >= 1 or gi >= 2:
                                    t.wait_ge(sems['y3'].h,
                                              y3base + 32 * hb + gi - 1)
                                for kg2 in range(2):
                                    mmi = t.matmul(
                                        po[gi % 2],
                                        w3s[hb % 2][:, kg2 * D + dg * 128:
                                            kg2 * D + (dg + 1) * 128],
                                        h2blk[:, kg2 * TOK + nb * 512:
                                              kg2 * TOK + (nb + 1) * 512],
                                        start=(kg2 == 0), stop=(kg2 == 1))
                                mmi.then_inc(sems['mm'].h, 1)
                sems['mm'].v += 36 * NHB

                @blk.scalar
                def _(s):
                    for hb in range(NHB):
                        for sub in range(2):
                            for nb in range(2):
                                gi = 2 * sub + nb
                                s.wait_ge(sems['mm'].h, mm0 + 36 * hb + gi + 1)
                                s.activation(
                                    h2blk[:, sub * TOK + nb * 512:
                                          sub * TOK + (nb + 1) * 512],
                                    ph2[gi % 2], AF.Relu,
                                    bias=biases[:, NB1 + hb * 2 + sub:
                                                NB1 + hb * 2 + sub + 1]) \
                                 .then_inc(sems['ev'].h, 1)
                sems['ev'].v += 4 * NHB

                @blk.vector
                def _(v):
                    for hb in range(NHB):
                        for dg in range(G):
                            for nb in range(2):
                                gi = 2 * dg + nb
                                v.wait_ge(sems['mm'].h,
                                          mm0 + 36 * hb + 4 + gi + 1)
                                sl = slice(dg * TOK + nb * 512,
                                           dg * TOK + (nb + 1) * 512)
                                v.tensor_add(y3acc[:, sl], y3acc[:, sl],
                                             po[gi % 2]) \
                                 .then_inc(sems['y3'].h, 1)
                sems['y3'].v += 32 * NHB

        with nc.sbuf_tensor("xm0", [128, TOK], F32) as xm0, \
             nc.sbuf_tensor("xm1", [128, TOK], F32) as xm1, \
             nc.sbuf_tensor("yc0", [128, TOK], F32) as yc0, \
             nc.sbuf_tensor("yc1", [128, TOK], F32) as yc1, \
             nc.sbuf_tensor("xv0", [128, TOK], F32) as xv0, \
             nc.sbuf_tensor("xv1", [128, TOK], F32) as xv1:
            xms = [xm0, xm1]
            ycs = [yc0, yc1]
            xvs = [xv0, xv1]
            v0 = sems['v'].v
            xm3 = xmid.rearrange("(g p) t -> g p t", p=128)
            xo3 = xo.rearrange("(g p) t -> g p t", p=128)
            ld = PPStream(pp_s)
            stm = PPStream(pp_o)
            with nc.Block() as blk:
                @blk.sync
                def _(e):
                    e.wait_ge(sems['spill'].h, 16)
                    for dg in range(G):
                        if dg >= 2:
                            e.wait_ge(sems['v'].h, v0 + dg - 1)
                        e.dma_start(out=xms[dg % 2][:, :],
                                    in_=xm3[dg, :, :]).then_inc(ld.inc(dg), 16)

                @blk.vector
                def _(v):
                    v.wait_ge(sems['ldb'].h, 48)
                    for dg in range(G):
                        ld.wait(v, dg)
                        if dg >= 2:
                            stm.wait(v, dg - 2)
                        v.tensor_scalar(
                            ycs[dg % 2][:, :],
                            y3acc[:, dg * TOK:(dg + 1) * TOK],
                            biases[:, 2 * NB1 + dg:2 * NB1 + dg + 1],
                            0.5, op0=ALU.add, op1=ALU.mult)
                        v.tensor_add(xvs[dg % 2][:, :], ycs[dg % 2][:, :],
                                     xms[dg % 2][:, :]).then_inc(sems['v'].h, 1)
                sems['v'].v += G

                @blk.gpsimd
                def _(g2):
                    for dg in range(G):
                        g2.wait_ge(sems['v'].h, v0 + dg + 1)
                        g2.dma_start(out=xo3[dg, :, :],
                                     in_=xvs[dg % 2][:, :]) \
                          .then_inc(stm.inc(dg), 16)

        es_y3.close()
        es_h1.close()

        xo_done = [(pp_o.s[i].h, pp_o.s[i].v) for i in range(2)]
        emit_ln_stream(nc, sems, pp_s, pp_o, xo, uo, TOK, ones_c[:, :],
                       ones_r[:, :], stat_row, wait_sems=xo_done)

        with nc.Block() as blk:
            final = [(pp_o.s[i].h, pp_o.s[i].v) for i in range(2)]

            @blk.sync
            def _(e):
                for hs, hv in final:
                    e.wait_ge(hs, hv)
    return nc



DH = 1024
GH = DH // 128


def build_scan():
    nc = bass.Bass("TRN2", target_bir_lowering=False,
                   detect_race_conditions=False)
    u_in = bf16ap(nc, "u", [D, SEQ])
    uown_in = bf16ap(nc, "uown", [DH, SEQ])
    wdt = f32ap(nc, "wdt", [D, DH])
    wb = f32ap(nc, "wb", [D, DH])
    wc = f32ap(nc, "wc", [D, DH])
    prm = f32ap(nc, "prm", [128, 416])
    yo = bf16ap(nc, "y", [DH, SEQ], kind="ExternalOutput")

    NPR = 416
    C_ARAW, C_BB, C_CB = 0, 128, 256
    C_DV, C_BDT, C_BBI, C_BCI = 384, 392, 400, 408

    st = ExitStack()
    with st:
        E = st.enter_context
        sems = {k: Sem(st, nc, "sem_" + k) for k in
                ["ldx", "mm", "evs", "evv", "sq", "v"]}
        pp_w = PP(st, nc, "ppw")
        pp_s = PP(st, nc, "pps")
        u_sb = E(nc.sbuf_tensor("u_sb", [128, G * SEQ], BF16))
        uo_sb = E(nc.sbuf_tensor("uo_sb", [128, GH * SEQ], BF16))
        prm_sb = E(nc.sbuf_tensor("prm_sb", [128, NPR], F32))
        A_sb = E(nc.sbuf_tensor("A_sb", [128, 128], F32))
        pt1 = E(nc.sbuf_tensor("pt1", [128, 128], F32))
        pt2 = E(nc.sbuf_tensor("pt2", [128, 128], F32))
        scr = E(nc.sbuf_tensor("scr", [128, 10 * SEQ], F32))
        SL = lambda i: scr[:, i * SEQ:(i + 1) * SEQ]
        SLn = lambda i, nb: scr[:, i * SEQ + nb * 512:i * SEQ + (nb + 1) * 512]
        AAB, RT, T1, T2, DT, BSB, CSB, W, YA, YB = range(10)
        AC0, AC1, BC, SS = T1, AAB, BSB, RT
        yt0 = E(nc.sbuf_tensor("yt0", [128, SEQ], BF16))
        yt1 = E(nc.sbuf_tensor("yt1", [128, SEQ], BF16))
        yts = [yt0, yt1]
        wbufA = E(nc.sbuf_tensor("wbufA", [128, G * 128], BF16))
        wbufB = E(nc.sbuf_tensor("wbufB", [128, G * 128], BF16))
        wbufs = [wbufA, wbufB]
        psA = E(nc.psum_tensor("psA", [128, 512], F32))
        psB = E(nc.psum_tensor("psB", [128, 512], F32))
        psums = [psA[:], psB[:]]
        waps = [wdt, wb, wc]

        plan = {}
        sc_n = 0
        vc_n = 0
        for gidx in range(GH * 12):
            wi = (gidx % 12) // 4
            if wi == 0:
                sc_n += 1
                plan[gidx] = ('evs', sc_n)
            else:
                vc_n += 1
                plan[gidx] = ('evv', vc_n)
        v_n = 0
        A_mark = 1
        v_n = 1
        dtstt_mark = {}
        scan_mark = {}
        yout_mark = {}
        for dg in range(GH):
            v_n += 1
            dtstt_mark[dg] = v_n
            for n in range(NST):
                v_n += 1
                scan_mark[(dg, n)] = v_n
            v_n += 1
            yout_mark[dg] = v_n
        s_n = 0
        asp_mark = 2
        s_n = 2
        ln_mark = {}
        a_mark = {}
        for dg in range(GH):
            s_n += 1
            ln_mark[dg] = s_n
            for n in range(NST):
                s_n += 1
                a_mark[(dg, n)] = s_n

        ldx_total = 16 * 3

        with nc.Block() as blk:
            @blk.sync
            def _(e):
                e.dma_start(out=u_sb[:].rearrange("p (g t) -> p g t", t=SEQ),
                            in_=u_in.rearrange("(g p) t -> p g t", p=128)) \
                 .then_inc(sems['ldx'].h, 16)
                e.dma_start(out=uo_sb[:].rearrange("p (g t) -> p g t", t=SEQ),
                            in_=uown_in.rearrange("(g p) t -> p g t", p=128)) \
                 .then_inc(sems['ldx'].h, 16)
                e.dma_start(out=prm_sb[:, :], in_=prm) \
                 .then_inc(sems['ldx'].h, 16)
                for dg in range(GH):
                    e.wait_ge(sems['v'].h, yout_mark[dg])
                    e.dma_start(
                        out=yo.rearrange("(dg p) t -> dg p t", p=128)[dg],
                        in_=yts[dg % 2][:, :]).then_inc(pp_s.issue(dg), 16)
                for i in range(2):
                    e.wait_ge(pp_s.s[i].h, pp_s.s[i].v)

            @blk.gpsimd
            def _(g2):
                for dg in range(GH):
                    for wi in range(3):
                        idx = dg * 3 + wi
                        if idx >= 2:
                            g2.wait_ge(sems['mm'].h, 4 * (idx - 1))
                        g2.dma_start(
                            out=wbufs[idx % 2][:].rearrange(
                                "p (kg m) -> p kg m", m=128),
                            in_=wslab_ap(waps[wi], G, dg)) \
                          .then_inc(pp_w.issue(idx), 16)

            @blk.tensor
            def _(t):
                t.wait_ge(sems['ldx'].h, ldx_total)
                for dg in range(GH):
                    for wi in range(3):
                        idx = dg * 3 + wi
                        s_ = pp_w.s[idx % 2]
                        t.wait_ge(s_.h, 16 * (idx // 2 + 1))
                        for nb in range(4):
                            gidx = dg * 12 + wi * 4 + nb
                            if gidx >= 2:
                                psem, pval = plan[gidx - 2]
                                t.wait_ge(sems[psem].h, pval)
                            for kg in range(G):
                                mmi = t.matmul(
                                    psums[gidx % 2],
                                    wbufs[idx % 2][:, kg * 128:(kg + 1) * 128],
                                    u_sb[:, kg * SEQ + nb * 512:
                                         kg * SEQ + (nb + 1) * 512],
                                    start=(kg == 0), stop=(kg == G - 1))
                            mmi.then_inc(sems['mm'].h, 1)
            sems['mm'].v += GH * 12

            @blk.scalar
            def _(s):
                s.wait_ge(sems['ldx'].h, ldx_total)
                s.activation(pt1[:, :], prm_sb[:, C_ARAW:C_ARAW + 128], AF.Abs)
                s.activation(pt2[:, :], pt1[:, :], AF.Exp, scale=-1.0)
                s.activation(pt1[:, :], pt2[:, :], AF.Ln,
                             bias=1.0).then_inc(sems['sq'].h, 1)
                s.activation(pt2[:, :], prm_sb[:, C_ARAW:C_ARAW + 128],
                             AF.Relu).then_inc(sems['sq'].h, 1)
                for dg in range(GH):
                    bcol = prm_sb[:, C_BDT + dg:C_BDT + dg + 1]
                    if dg >= 1:
                        s.wait_ge(sems['v'].h, yout_mark[dg - 1])
                    for nb in range(4):
                        gidx = dg * 12 + nb
                        s.wait_ge(sems['mm'].h, gidx + 1)
                        s.activation(SLn(AAB, nb), psums[gidx % 2], AF.Abs,
                                     bias=bcol)
                        s.activation(SLn(RT, nb), psums[gidx % 2], AF.Relu,
                                     bias=bcol).then_inc(sems['evs'].h, 1)
                    s.activation(SL(T1), SL(AAB), AF.Exp, scale=-1.0)
                    s.activation(SL(T2), SL(T1), AF.Ln,
                                 bias=1.0).then_inc(sems['sq'].h, 1)
                    for n in range(NST):
                        if n == 0:
                            s.wait_ge(sems['v'].h, dtstt_mark[dg])
                        if n >= 2:
                            s.wait_ge(sems['v'].h, scan_mark[(dg, n - 2)])
                        acbuf = AC0 if n % 2 == 0 else AC1
                        s.activation(SL(acbuf), SL(DT), AF.Exp,
                                     scale=A_sb[:, dg * NST + n:
                                                dg * NST + n + 1]) \
                         .then_inc(sems['sq'].h, 1)

            @blk.vector
            def _(v):
                v.wait_ge(sems['ldx'].h, ldx_total)
                v.wait_ge(sems['sq'].h, asp_mark)
                v.scalar_tensor_tensor(A_sb[:, :], pt1[:, :], 1.0, pt2[:, :],
                                       op0=ALU.bypass, op1=ALU.add)
                v.tensor_scalar_mul(A_sb[:, :], A_sb[:, :], -1.0) \
                 .then_inc(sems['v'].h, 1)
                for dg in range(GH):
                    for wi in (1, 2):
                        col = prm_sb[:, (C_BBI if wi == 1 else C_BCI) + dg:
                                     (C_BBI if wi == 1 else C_BCI) + dg + 1]
                        dst = BSB if wi == 1 else CSB
                        for nb in range(4):
                            gidx = dg * 12 + wi * 4 + nb
                            v.wait_ge(sems['mm'].h, gidx + 1)
                            v.tensor_scalar(SLn(dst, nb), psums[gidx % 2],
                                            col, None, op0=ALU.add) \
                             .then_inc(sems['evv'].h, 1)
                    v.wait_ge(sems['sq'].h, ln_mark[dg])
                    v.scalar_tensor_tensor(SL(DT), SL(T2), 1e-4, SL(RT),
                                           op0=ALU.add, op1=ALU.add) \
                     .then_inc(sems['v'].h, 1)
                    v.tensor_mul(SL(W), SL(DT), SL(BSB))
                    v.tensor_mul(SL(W), SL(W),
                                 uo_sb[:, dg * SEQ:(dg + 1) * SEQ])
                    for n in range(NST):
                        v.wait_ge(sems['sq'].h, a_mark[(dg, n)])
                        acbuf = AC0 if n % 2 == 0 else AC1
                        v.tensor_scalar_mul(
                            SL(BC), SL(W),
                            prm_sb[:, C_BB + dg * NST + n:
                                   C_BB + dg * NST + n + 1])
                        v.tensor_tensor_scan(SL(SS), SL(acbuf), SL(BC), 0.0,
                                             op0=ALU.mult, op1=ALU.add)
                        ccol = prm_sb[:, C_CB + dg * NST + n:
                                      C_CB + dg * NST + n + 1]
                        if n == 0:
                            v.tensor_scalar(SL(YA), SL(SS), ccol, None,
                                            op0=ALU.mult) \
                             .then_inc(sems['v'].h, 1)
                        else:
                            src_, dst_ = (YA, YB) if n % 2 == 1 else (YB, YA)
                            v.scalar_tensor_tensor(SL(dst_), SL(SS), ccol,
                                                   SL(src_), op0=ALU.mult,
                                                   op1=ALU.add) \
                             .then_inc(sems['v'].h, 1)
                    ylast = YB if (NST - 1) % 2 == 1 else YA
                    v.tensor_mul(SL(T2), SL(CSB), SL(ylast))
                    if dg >= 2:
                        s_ = pp_s.s[dg % 2]
                        v.wait_ge(s_.h, 16 * ((dg - 2) // 2 + 1))
                    v.scalar_tensor_tensor(
                        yts[dg % 2][:, :],
                        uo_sb[:, dg * SEQ:(dg + 1) * SEQ],
                        prm_sb[:, C_DV + dg:C_DV + dg + 1], SL(T2),
                        op0=ALU.mult, op1=ALU.add).then_inc(sems['v'].h, 1)
            sems['v'].v += v_n
            sems['sq'].v += s_n
    return nc


def prep_scan_weights(inputs):
    out = []
    araw, bbase, cbase = (inputs["L0_attn_a_raw"], inputs["L0_attn_B_base"],
                          inputs["L0_attn_C_base"])
    dvec = inputs["L0_attn_D"]
    for dh in range(2):
        sl = slice(dh * DH, (dh + 1) * DH)
        c = {}
        c["wdt"] = np.ascontiguousarray(inputs["L0_attn_Wdt"][:, sl])
        c["wb"] = np.ascontiguousarray(inputs["L0_attn_Wb"][:, sl])
        c["wc"] = np.ascontiguousarray(inputs["L0_attn_Wc"][:, sl])
        prm = np.zeros((128, 416), np.float32)
        rearr = lambda a: a[sl].reshape(8, 128, 16).transpose(1, 0, 2) \
            .reshape(128, 128)
        prm[:, 0:128] = rearr(araw)
        prm[:, 128:256] = rearr(bbase)
        prm[:, 256:384] = rearr(cbase)
        prm[:, 384:392] = dvec[sl].reshape(8, 128).T
        prm[:, 392:400] = inputs["L0_attn_bdt"][sl].reshape(8, 128).T
        prm[:, 400:408] = inputs["L0_attn_bb"][sl].reshape(8, 128).T
        prm[:, 408:416] = inputs["L0_attn_bc"][sl].reshape(8, 128).T
        c["prm"] = prm
        out.append(c)
    return out


def run_scan(u_full_by_batch, scan_wc):
    if "scan" not in _cache:
        _cache["scan"] = build_scan()
    nc = _cache["scan"]
    in_maps = []
    for ci in range(NCORE):
        b, dh = divmod(ci, 2)
        m = dict(scan_wc[dh])
        m["u"] = u_full_by_batch[b]
        m["uown"] = np.ascontiguousarray(
            u_full_by_batch[b][dh * DH:(dh + 1) * DH])
        in_maps.append(m)
    res = run_bass_kernel_spmd(nc, in_maps, core_ids=list(range(NCORE)))
    ys = [np.asarray(r["y"]) for r in res.results]
    return [(ys[2 * b], ys[2 * b + 1]) for b in range(4)]



NMG1 = H1 // 128


def build_ffn1():
    nc = bass.Bass("TRN2", target_bir_lowering=False,
                   detect_race_conditions=False)
    xin = f32ap(nc, "x", [D, TOK])
    uln = bf16ap(nc, "uln", [D, TOK])
    W1T = f32ap(nc, "w1t", [D, H1])
    W2T = f32ap(nc, "w2t", [D, H1])
    W3T = f32ap(nc, "w3t", [H1, D])
    xo = f32ap(nc, "xo", [D, TOK], kind="ExternalOutput")
    uo = bf16ap(nc, "uo", [D, TOK], kind="ExternalOutput")
    NW = float(H1 * D)

    st = ExitStack()
    with st:
        E = st.enter_context
        sems = {k: Sem(st, nc, "sem_" + k) for k in
                ["ldx", "mm", "evs", "evv", "sq", "y3", "v", "s", "q"]}
        pp_w = PP(st, nc, "ppw")
        pp_g2 = PP(st, nc, "ppg2")
        pp_g3 = PP(st, nc, "ppg3")
        pp_s = PP(st, nc, "pps")
        pp_o = PP(st, nc, "ppo")
        ones_c = E(nc.sbuf_tensor("ones_c", [128, 1], F32))
        ones_r = E(nc.sbuf_tensor("ones_r", [1, 128], F32))
        ones_m = E(nc.sbuf_tensor("ones_m", [128, 128], F32))
        stat_row = E(nc.sbuf_tensor("stat_row", [1, 8 * 512], F32))
        thr_sb = E(nc.sbuf_tensor("thr_sb", [128, 8], F32))
        acc_sb = E(nc.sbuf_tensor("acc_sb", [128, 4], F32))
        xln = E(nc.sbuf_tensor("xln", [128, G * TOK], BF16))

        with nc.sbuf_tensor("gb0", [128, H1], F32) as gb0, \
             nc.sbuf_tensor("gb1", [128, H1], F32) as gb1, \
             nc.sbuf_tensor("gpart", [128, 80], F32) as gpart, \
             nc.psum_tensor("psg", [128, 512], F32) as psg:
            gbs = [gb0, gb1]
            slabs = []
            for wi, nkg, ncol in ((0, G, H1), (1, G, H1), (2, NMG1, D)):
                for kg in range(nkg):
                    slabs.append((wi, kg, ncol))
            NSL = len(slabs)
            srcs = [W1T, W2T, W3T]
            ld = PPStream(pp_w)
            v0 = sems['v'].v
            with nc.Block() as blk:
                @blk.sync
                def _(e):
                    e.dma_start(
                        out=xln[:].rearrange("p (g t) -> p g t", t=TOK),
                        in_=uln.rearrange("(g p) t -> p g t", p=128)) \
                     .then_inc(sems['ldx'].h, 16)
                sems['ldx'].v += 16

                @blk.gpsimd
                def _(g2):
                    g2.memset(ones_c[:, :], 1.0)
                    g2.memset(ones_r[:, :], 1.0)
                    g2.memset(ones_m[:, :], 1.0)
                    for i, (wi, kg, ncol) in enumerate(slabs):
                        if i >= 2:
                            g2.wait_ge(sems['v'].h, v0 + i - 1)
                        g2.dma_start(
                            out=gbs[i % 2][:, 0:ncol],
                            in_=srcs[wi].rearrange("(kg p) m -> kg p m",
                                                   p=128)[kg]) \
                          .then_inc(ld.inc(i), 16)

                @blk.vector
                def _(v):
                    for i, (wi, kg, ncol) in enumerate(slabs):
                        ld.wait(v, i)
                        v.reduce_sum(gpart[:, i:i + 1], gbs[i % 2][:, 0:ncol],
                                     axis=AX.X, apply_absolute_value=True) \
                         .then_inc(sems['v'].h, 1)
                    v.tensor_copy(gbs[0][:, 0:1024], gbs[1][:, 0:1024])
                    v.reduce_sum(acc_sb[:, 0:1], gpart[:, 0:G], axis=AX.X)
                    v.reduce_sum(acc_sb[:, 1:2], gpart[:, G:2 * G], axis=AX.X)
                    v.reduce_sum(acc_sb[:, 2:3], gpart[:, 2 * G:2 * G + NMG1],
                                 axis=AX.X)
                    v.tensor_copy(gbs[0][:, 0:1024], gbs[1][:, 0:1024])
                sems['v'].v += NSL
                vdone = sems['v'].v

            with nc.Block() as blk:
                @blk.tensor
                def _(t):
                    for wi in range(3):
                        t.matmul(psg[:, wi:wi + 1], ones_m[:, :],
                                 acc_sb[:, wi:wi + 1], start=True, stop=True)
            with nc.Block() as blk:
                @blk.vector
                def _(v):
                    for wi in range(3):
                        v.tensor_scalar(thr_sb[:, 2 * wi:2 * wi + 1],
                                        psg[:, wi:wi + 1], 0.5 / NW, 0.5e-8,
                                        op0=ALU.mult, op1=ALU.add)
                    v.tensor_copy(gb0[:, 0:512], gb1[:, 0:512])
                    v.tensor_copy(gb0[:, 0:512], gb1[:, 0:512])
                    for wi in range(3):
                        v.tensor_scalar_mul(thr_sb[:, 2 * wi + 1:2 * wi + 2],
                                            thr_sb[:, 2 * wi:2 * wi + 1],
                                            -1.0)
                    v.tensor_copy(gb0[:, 0:512], gb1[:, 0:512])
                    v.tensor_copy(gb0[:, 0:512], gb1[:, 0:512])

        es_y3 = ExitStack()
        y3acc = es_y3.enter_context(
            nc.sbuf_tensor("y3acc", [128, G * TOK], F32))
        with nc.sbuf_tensor("w1c0", [128, D], F32) as w1c0, \
             nc.sbuf_tensor("w1c1", [128, D], F32) as w1c1, \
             nc.sbuf_tensor("w2c0", [128, D], F32) as w2c0, \
             nc.sbuf_tensor("w2c1", [128, D], F32) as w2c1, \
             nc.sbuf_tensor("w3r0", [128, D], F32) as w3r0, \
             nc.sbuf_tensor("q1s", [128, D], BF16) as q1s, \
             nc.sbuf_tensor("q2s", [128, D], BF16) as q2s, \
             nc.sbuf_tensor("qt", [128, D], F32) as qt, \
             nc.sbuf_tensor("qt2", [128, D], F32) as qt2, \
             nc.sbuf_tensor("q3s", [128, 4 * D], BF16) as q3s, \
             nc.sbuf_tensor("hid4", [128, 4 * TOK], BF16) as hid4, \
             nc.sbuf_tensor("sg", [128, TOK], F32) as sg, \
             nc.psum_tensor("px1a", [128, 512], F32) as px1a, \
             nc.psum_tensor("px1b", [128, 512], F32) as px1b, \
             nc.psum_tensor("px2a", [128, 512], F32) as px2a, \
             nc.psum_tensor("px2b", [128, 512], F32) as px2b, \
             nc.psum_tensor("poa", [128, 512], F32) as poa, \
             nc.psum_tensor("pob", [128, 512], F32) as pob:
            w1c = [w1c0, w1c1]
            w2c = [w2c0, w2c1]
            w3r = [w3r0, w3r0]
            px1 = [px1a[:], px1b[:]]
            px2 = [px2a[:], px2b[:]]
            po = [poa[:], pob[:]]
            ld1 = PPStream(pp_w)
            ld2 = PPStream(pp_g2)
            ld3 = PPStream(pp_g3)
            mm0 = sems['mm'].v
            evs0 = sems['evs'].v
            evv0 = sems['evv'].v
            y30 = sems['y3'].v
            q0 = sems['q'].v
            octs = [(o * 4, min(4, NMG1 - o * 4)) for o in range((NMG1 + 3) // 4)]
            def mm_at(mg):
                full_oct = (mg + 1) // 4
                return 4 * (mg + 1) + 32 * full_oct
            def w3mm_at(oi):
                o0, osz = octs[oi]
                return 4 * (o0 + osz) + 32 * oi
            qmark = {}
            qn = q0
            for mg in range(NMG1):
                qn += 1
                qmark[("q12", mg)] = qn
                qn += 1
                qmark[("q3", mg)] = qn

            with nc.Block() as blk:
                @blk.gpsimd
                def _(g2):
                    g2.memset(y3acc[:, :], 0.0).then_inc(sems['y3'].h, 1)
                    for mg in range(NMG1):
                        if mg >= 1:
                            g2.wait_ge(sems['q'].h, qmark[("q3", mg - 1)])
                        g2.dma_start(out=w1c[mg % 2][:].rearrange(
                                         "p (kg m) -> p kg m", m=128),
                                     in_=wslab_ap(W1T, G, mg)) \
                          .then_inc(ld1.inc(mg), 16)
                        g2.dma_start(out=w2c[mg % 2][:].rearrange(
                                         "p (kg m) -> p kg m", m=128),
                                     in_=wslab_ap(W2T, G, mg)) \
                          .then_inc(ld2.inc(mg), 16)
                        g2.dma_start(out=w3r[mg % 2][:, :],
                                     in_=W3T.rearrange("(kg p) m -> kg p m",
                                                       p=128)[mg]) \
                          .then_inc(ld3.inc(mg), 16)
                sems['y3'].v += 1
                y3base = y30 + 1

                @blk.vector
                def _(v):
                    v.wait_ge(sems['ldx'].h, sems['ldx'].v)
                    for oi, (o0, osz) in enumerate(octs):
                        for j in range(osz):
                            mg = o0 + j
                            ld1.wait(v, mg)
                            ld2.wait(v, mg)
                            if mg >= 1:
                                v.wait_ge(sems['mm'].h, mm_at(mg - 1))
                            for (wc_, qs_, tc_) in ((w1c, q1s, 0),
                                                    (w2c, q2s, 2)):
                                v.tensor_scalar(
                                    qt[:, :], wc_[mg % 2][:, :],
                                    thr_sb[:, tc_:tc_ + 1], None,
                                    op0=ALU.is_gt)
                                v.tensor_scalar(
                                    qt2[:, :], wc_[mg % 2][:, :],
                                    thr_sb[:, tc_ + 1:tc_ + 2], None,
                                    op0=ALU.is_lt)
                                qi_ = v.tensor_sub(qs_[:, :], qt[:, :],
                                                   qt2[:, :])
                            qi_.then_inc(sems['q'].h, 1)
                            ld3.wait(v, mg)
                            if oi >= 1:
                                v.wait_ge(sems['mm'].h, w3mm_at(oi - 1) + 32)
                            v.tensor_scalar(qt[:, :], w3r[mg % 2][:, :],
                                            thr_sb[:, 4:5], None,
                                            op0=ALU.is_gt)
                            v.tensor_scalar(qt2[:, :], w3r[mg % 2][:, :],
                                            thr_sb[:, 5:6], None,
                                            op0=ALU.is_lt)
                            v.tensor_sub(q3s[:, j * D:(j + 1) * D], qt[:, :],
                                         qt2[:, :]) \
                             .then_inc(sems['q'].h, 1)
                            for nb in range(2):
                                v.wait_ge(sems['evs'].h,
                                          evs0 + 2 * mg + nb + 1)
                                v.wait_ge(sems['mm'].h,
                                          4 * mg + 32 * (mg // 4) + 2 +
                                          nb + 1)
                                hm = v.tensor_mul(
                                    hid4[:, j * TOK + nb * 512:
                                         j * TOK + (nb + 1) * 512],
                                    sg[:, nb * 512:(nb + 1) * 512],
                                    px2[nb])
                                if j == osz - 1 and nb == 1:
                                    v.tensor_copy(qt[:, 0:256], qt[:, 256:512])
                                    v.tensor_copy(qt[:, 0:256],
                                                  qt[:, 256:512]) \
                                     .then_inc(sems['evv'].h, 1)
                                else:
                                    hm.then_inc(sems['evv'].h, 1)
                        for dg in range(G):
                            for nb in range(2):
                                gi = 2 * dg + nb
                                v.wait_ge(sems['mm'].h,
                                          w3mm_at(oi) + gi + 1)
                                sl = slice(dg * TOK + nb * 512,
                                           dg * TOK + (nb + 1) * 512)
                                v.tensor_add(y3acc[:, sl], y3acc[:, sl],
                                             po[gi % 2]) \
                                 .then_inc(sems['y3'].h, 1)
                sems['q'].v = qn
                sems['evv'].v += 2 * NMG1
                sems['y3'].v += 32 * len(octs)

                @blk.tensor
                def _(t):
                    for oi, (o0, osz) in enumerate(octs):
                        for j in range(osz):
                            mg = o0 + j
                            t.wait_ge(sems['q'].h, qmark[("q12", mg)])
                            for xi, (qs_, ps_) in enumerate(((q1s, px1),
                                                            (q2s, px2))):
                                for nb in range(2):
                                    if mg >= 1:
                                        gsem = sems['evs'] if xi == 0 \
                                            else sems['evv']
                                        base = evs0 if xi == 0 else evv0
                                        t.wait_ge(gsem.h,
                                                  base + 2 * (mg - 1) + nb + 1)
                                    for kg in range(G):
                                        mmi = t.matmul(
                                            ps_[nb],
                                            qs_[:, kg * 128:(kg + 1) * 128],
                                            xln[:, kg * TOK + nb * 512:
                                                kg * TOK + (nb + 1) * 512],
                                            start=(kg == 0),
                                            stop=(kg == G - 1))
                                    mmi.then_inc(sems['mm'].h, 1)
                        for dg in range(G):
                            for nb in range(2):
                                gi = 2 * dg + nb
                                t.wait_ge(sems['q'].h,
                                          qmark[("q3", o0 + osz - 1)])
                                t.wait_ge(sems['evv'].h,
                                          evv0 + 2 * (o0 + osz - 1) + 2)
                                if oi >= 1 or gi >= 2:
                                    t.wait_ge(sems['y3'].h,
                                              y3base + 32 * oi + gi - 1)
                                for j in range(osz):
                                    mmi = t.matmul(
                                        po[gi % 2],
                                        q3s[:, j * D + dg * 128:
                                            j * D + (dg + 1) * 128],
                                        hid4[:, j * TOK + nb * 512:
                                             j * TOK + (nb + 1) * 512],
                                        start=(j == 0), stop=(j == osz - 1))
                                mmi.then_inc(sems['mm'].h, 1)

                @blk.scalar
                def _(s):
                    for mg in range(NMG1):
                        for nb in range(2):
                            s.wait_ge(sems['mm'].h, 4 * mg + 32 * (mg // 4) +
                                      nb + 1)
                            if mg >= 1:
                                s.wait_ge(sems['evv'].h,
                                          evv0 + 2 * (mg - 1) + nb + 1)
                            s.activation(sg[:, nb * 512:(nb + 1) * 512],
                                         px1[nb], AF.Sigmoid) \
                             .then_inc(sems['evs'].h, 1)
                sems['evs'].v += 2 * NMG1
                sems['mm'].v = mm0 + 4 * NMG1 + 32 * len(octs)

        with nc.sbuf_tensor("xm0", [128, TOK], F32) as xm0, \
             nc.sbuf_tensor("xm1", [128, TOK], F32) as xm1, \
             nc.sbuf_tensor("xv0", [128, TOK], F32) as xv0, \
             nc.sbuf_tensor("xv1", [128, TOK], F32) as xv1:
            xms = [xm0, xm1]
            xvs = [xv0, xv1]
            v0 = sems['v'].v
            xm3 = xin.rearrange("(g p) t -> g p t", p=128)
            xo3 = xo.rearrange("(g p) t -> g p t", p=128)
            ld = PPStream(pp_s)
            stm = PPStream(pp_o)
            with nc.Block() as blk:
                @blk.sync
                def _(e):
                    for dg in range(G):
                        if dg >= 2:
                            e.wait_ge(sems['v'].h, v0 + dg - 1)
                        e.dma_start(out=xms[dg % 2][:, :],
                                    in_=xm3[dg, :, :]).then_inc(ld.inc(dg), 16)

                @blk.vector
                def _(v):
                    for dg in range(G):
                        ld.wait(v, dg)
                        if dg >= 2:
                            stm.wait(v, dg - 2)
                        v.scalar_tensor_tensor(
                            xvs[dg % 2][:, :],
                            y3acc[:, dg * TOK:(dg + 1) * TOK], 0.5,
                            xms[dg % 2][:, :],
                            op0=ALU.mult, op1=ALU.add).then_inc(sems['v'].h, 1)
                sems['v'].v += G

                @blk.gpsimd
                def _(g2):
                    for dg in range(G):
                        g2.wait_ge(sems['v'].h, v0 + dg + 1)
                        g2.dma_start(out=xo3[dg, :, :],
                                     in_=xvs[dg % 2][:, :]) \
                          .then_inc(stm.inc(dg), 16)

        es_y3.close()

        xo_done = [(pp_o.s[i].h, pp_o.s[i].v) for i in range(2)]
        emit_ln_stream(nc, sems, pp_s, pp_o, xo, uo, TOK, ones_c[:, :],
                       ones_r[:, :], stat_row, wait_sems=xo_done)

        with nc.Block() as blk:
            final = [(pp_o.s[i].h, pp_o.s[i].v) for i in range(2)]

            @blk.sync
            def _(e):
                for hs, hv in final:
                    e.wait_ge(hs, hv)
    return nc


def prep_ffn1_weights(inputs):
    c = {}
    c["w1t"] = np.ascontiguousarray(inputs["L1_ffn_w1"].T)
    c["w2t"] = np.ascontiguousarray(inputs["L1_ffn_w2"].T)
    c["w3t"] = np.ascontiguousarray(inputs["L1_ffn_w3"].T)
    return c


def run_ffn1(x_shards, u_shards, wc):
    if "ffn1" not in _cache:
        _cache["ffn1"] = build_ffn1()
    nc = _cache["ffn1"]
    in_maps = []
    for ci in range(NCORE):
        m = dict(wc)
        m["x"] = x_shards[ci]
        m["uln"] = u_shards[ci]
        in_maps.append(m)
    res = run_bass_kernel_spmd(nc, in_maps, core_ids=list(range(NCORE)))
    xo = [np.asarray(r["xo"]) for r in res.results]
    uo = [np.asarray(r["uo"]) for r in res.results]
    return xo, uo



EXT = TOK + 256
NQB = TOK // 128
NH = 16
M = 4096


ATTN_DEBUG = False


def build_attn():
    nc = bass.Bass("TRN2", target_bir_lowering=False,
                   detect_race_conditions=False)
    xin = f32ap(nc, "x", [D, TOK])
    uext = bf16ap(nc, "uext", [D, EXT])
    wqt = f32ap(nc, "wqt", [D, D])
    wkt = f32ap(nc, "wkt", [D, D])
    wvt = f32ap(nc, "wvt", [D, D])
    wot = f32ap(nc, "wot", [D, D])
    wkn = f32ap(nc, "wkn", [D, D])
    membt = f32ap(nc, "membt", [D, M])
    memb = f32ap(nc, "memb", [M, D])
    gw = f32ap(nc, "gw", [D, 128])
    gb = f32ap(nc, "gb", [128, 1])
    masks = f32ap(nc, "masks", [128, NQB * 384])
    ident = f32ap(nc, "ident", [128, 128])
    xo = f32ap(nc, "xo", [D, TOK], kind="ExternalOutput")
    uo = bf16ap(nc, "uo", [D, TOK], kind="ExternalOutput")
    xk_hbm = nc.dram_tensor("xk_hbm", [D, TOK], BF16,
                            kind="ExternalOutput").ap()
    om_hbm = nc.dram_tensor("om_hbm", [D, TOK], BF16,
                            kind="ExternalOutput").ap()
    SCL = float(1.0 / np.sqrt(128.0))
    SCD = float(1.0 / np.sqrt(2048.0))

    st = ExitStack()
    with st:
        E = st.enter_context
        sems = {k: Sem(st, nc, "sem_" + k) for k in
                ["ldx", "ldg", "mm", "ev", "sq", "v", "s", "tp", "y3"]}
        pp_w = PP(st, nc, "ppw")
        pp_g = PP(st, nc, "ppg")
        pp_s = PP(st, nc, "pps")
        pp_s2 = PP(st, nc, "ps2")
        pp_o = PP(st, nc, "ppo")
        ones_c = E(nc.sbuf_tensor("ones_c", [128, 1], F32))
        ones_r = E(nc.sbuf_tensor("ones_r", [1, 128], F32))
        idn = E(nc.sbuf_tensor("idn", [128, 128], BF16))
        gb_sb = E(nc.sbuf_tensor("gb_sb", [128, 1], F32))
        mask_sb = E(nc.sbuf_tensor("mask_sb", [128, NQB * 384], F32))
        stat_row = E(nc.sbuf_tensor("stat_row", [1, 8 * 512], F32))
        m8 = E(nc.sbuf_tensor("m8", [128, 8], BF16))
        redm = E(nc.sbuf_tensor("redm", [128, 8], F32))
        wsA = E(nc.sbuf_tensor("wsA", [128, G * 128], BF16))
        wsB = E(nc.sbuf_tensor("wsB", [128, G * 128], BF16))
        wbufs = [wsA, wsB]
        es_ps = ExitStack()
        psA = es_ps.enter_context(nc.psum_tensor("psA", [128, 512], F32))
        psB = es_ps.enter_context(nc.psum_tensor("psB", [128, 512], F32))
        psums = [psA[:], psB[:]]
        widx = [0]

        with nc.Block() as blk:
            @blk.sync
            def _(e):
                e.dma_start(out=mask_sb[:, :], in_=masks) \
                 .then_inc(sems['ldx'].h, 16)
                e.dma_start(out=gb_sb[:, :], in_=gb) \
                 .then_inc(sems['ldx'].h, 16)
            sems['ldx'].v += 32

            @blk.gpsimd
            def _(g2):
                g2.memset(ones_c[:, :], 1.0)
                g2.memset(ones_r[:, :], 1.0)
                g2.dma_start(out=idn[:, :], in_=ident) \
                  .then_inc(sems['ldg'].h, 16)
            sems['ldg'].v += 16

        def proj_phase(wsrc, rhs_sb, rhs_base, rhs_tot, chunks, evac,
                       extra_tensor=None, extra_emit=None):
            mm0 = sems['mm'].v
            ev0 = sems['ev'].v
            nch = len(chunks)
            w0 = widx[0]
            with nc.Block() as blk:
                @blk.gpsimd
                def _(g2):
                    for dg in range(G):
                        i = w0 + dg
                        if dg >= 2:
                            g2.wait_ge(sems['mm'].h, mm0 + nch * (dg - 1))
                        g2.dma_start(
                            out=wbufs[i % 2][:].rearrange(
                                "p (kg m) -> p kg m", m=128),
                            in_=wslab_ap(wsrc, G, dg)) \
                          .then_inc(pp_w.issue(i), 16)

                @blk.tensor
                def _(t):
                    for dg in range(G):
                        i = w0 + dg
                        s_ = pp_w.s[i % 2]
                        t.wait_ge(s_.h, 16 * (i // 2 + 1))
                        for ci, (c0, w) in enumerate(chunks):
                            gi = nch * dg + ci
                            if gi >= 2:
                                t.wait_ge(sems['ev'].h, ev0 + gi - 1)
                            for kg in range(G):
                                mmi = t.matmul(
                                    psums[gi % 2][:, 0:w],
                                    wbufs[i % 2][:, kg * 128:(kg + 1) * 128],
                                    rhs_sb[:, kg * rhs_tot + rhs_base + c0:
                                           kg * rhs_tot + rhs_base + c0 + w],
                                    start=(kg == 0), stop=(kg == G - 1))
                            mmi.then_inc(sems['mm'].h, 1)
                        if extra_tensor is not None:
                            extra_tensor(t, dg)

                @blk.scalar
                def _(s):
                    for dg in range(G):
                        for ci in range(nch):
                            gi = nch * dg + ci
                            s.wait_ge(sems['mm'].h, mm0 + gi + 1)
                            evac(s, dg, ci,
                                 psums[gi % 2][:, 0:chunks[ci][1]]) \
                             .then_inc(sems['ev'].h, 1)
                if extra_emit is not None:
                    extra_emit(blk)
            sems['mm'].v += nch * G
            sems['ev'].v += nch * G
            widx[0] += G

        CH_OWN = [(256, 512), (768, 512)]
        CH_EXT = [(0, 512), (512, 512), (1024, 256)]

        es_q = ExitStack()
        qol = es_q.enter_context(nc.sbuf_tensor("qol", [128, G * TOK], BF16))
        es_kv = ExitStack()
        k_sb = es_kv.enter_context(
            nc.sbuf_tensor("k_sb", [128, G * EXT], BF16))
        vT_sb = es_kv.enter_context(
            nc.sbuf_tensor("vT_sb", [128, G * 10 * 128], BF16))

        stx = PPStream(pp_s2)
        with nc.sbuf_tensor("hs_sb", [128, G * EXT], BF16) as hs_sb, \
             nc.sbuf_tensor("xkt0", [128, TOK], BF16) as xkt0, \
             nc.sbuf_tensor("xkt1", [128, TOK], BF16) as xkt1, \
             nc.sbuf_tensor("vt0", [128, 512], BF16) as vt0, \
             nc.sbuf_tensor("vt1", [128, 512], BF16) as vt1, \
             nc.psum_tensor("pstA", [128, 128], BF16) as pstA, \
             nc.psum_tensor("pstB", [128, 128], BF16) as pstB:
            xkts = [xkt0, xkt1]
            vts = [vt0, vt1]
            pst = [pstA[:], pstB[:]]
            with nc.Block() as blk:
                @blk.sync
                def _(e):
                    e.dma_start(
                        out=hs_sb[:].rearrange("p (g t) -> p g t", t=EXT),
                        in_=uext.rearrange("(g p) t -> p g t", p=128)) \
                     .then_inc(sems['ldx'].h, 16)
                sems['ldx'].v += 16
                ldx_all = sems['ldx'].v
                ldg_all = sems['ldg'].v

                @blk.tensor
                def _(t):
                    t.wait_ge(sems['ldx'].h, ldx_all)
                    t.wait_ge(sems['ldg'].h, ldg_all)

            ev_b = sems['ev'].v

            def evac_xk(s, dg, ci, ps):
                if dg >= 2 and ci == 0:
                    s_ = pp_s2.s[dg % 2]
                    s.wait_ge(s_.h,
                              stx.base[dg % 2] + 16 * ((dg - 2) // 2 + 1))
                return s.activation(
                    xkts[dg % 2][:, ci * 512:(ci + 1) * 512], ps, AF.Copy,
                    scale=SCD)

            def extra_xk(blk):
                @blk.sync
                def _(e):
                    for dg in range(G):
                        e.wait_ge(sems['ev'].h, ev_b + 2 * dg + 2)
                        e.dma_start(
                            out=xk_hbm.rearrange("(g p) t -> g p t",
                                                 p=128)[dg],
                            in_=xkts[dg % 2][:, :]).then_inc(stx.inc(dg), 16)

            proj_phase(wkn, hs_sb, 0, EXT, CH_OWN, evac_xk,
                       extra_emit=extra_xk)

            def evac_q(s, dg, ci, ps):
                return s.activation(
                    qol[:, dg * TOK + ci * 512:dg * TOK + (ci + 1) * 512],
                    ps, AF.Copy, scale=SCL)
            proj_phase(wqt, hs_sb, 0, EXT, CH_OWN, evac_q)

            def evac_k(s, dg, ci, ps):
                c0, w = CH_EXT[ci]
                return s.activation(
                    k_sb[:, dg * EXT + c0:dg * EXT + c0 + w], ps, AF.Copy)
            proj_phase(wkt, hs_sb, 0, EXT, CH_EXT, evac_k)

            blocks_of = [4, 4, 2]
            tpc_after = []
            tot = 0
            for dg in range(G):
                for ci in range(3):
                    tot += blocks_of[ci]
                    tpc_after.append(tot)
            tp0 = sems['tp'].v
            vcp0 = sems['v'].v
            ev_v0 = sems['ev'].v

            def evac_v(s, dg, ci, ps):
                gi = 3 * dg + ci
                if gi >= 2:
                    s.wait_ge(sems['tp'].h, tp0 + tpc_after[gi - 2])
                c0, w = CH_EXT[ci]
                return s.activation(vts[gi % 2][:, 0:w], ps, AF.Copy)

            def extra_v(t, dg):
                for ci in range(3):
                    gi = 3 * dg + ci
                    c0, w = CH_EXT[ci]
                    t.wait_ge(sems['ev'].h, ev_v0 + gi + 1)
                    base_t = tpc_after[gi] - blocks_of[ci]
                    for j in range(blocks_of[ci]):
                        n_tp = base_t + j
                        if n_tp >= 2:
                            t.wait_ge(sems['v'].h, vcp0 + n_tp - 1)
                        t.transpose(pst[n_tp % 2][:, :],
                                    vts[gi % 2][:, j * 128:(j + 1) * 128],
                                    idn[:, :]).then_inc(sems['tp'].h, 1)

            def extra_v_emit(blk):
                @blk.vector
                def _(v):
                    cnt = 0
                    for dg in range(G):
                        for ci in range(3):
                            gi = 3 * dg + ci
                            c0, w = CH_EXT[ci]
                            for j in range(blocks_of[ci]):
                                cnt += 1
                                v.wait_ge(sems['tp'].h, tp0 + cnt)
                                blk_i = c0 // 128 + j
                                v.tensor_copy(
                                    vT_sb[:, (dg * 10 + blk_i) * 128:
                                          (dg * 10 + blk_i + 1) * 128],
                                    pst[(cnt - 1) % 2][:, :]) \
                                 .then_inc(sems['v'].h, 1)

            proj_phase(wvt, hs_sb, 0, EXT, CH_EXT, evac_v,
                       extra_tensor=extra_v, extra_emit=extra_v_emit)
            sems['tp'].v += 160
            sems['v'].v += 160

        with nc.sbuf_tensor("ctxT", [128, G * TOK], BF16) as ctxT:
            with nc.sbuf_tensor("sscr0", [128, 384], F32) as sscr0, \
                 nc.sbuf_tensor("sscr1", [128, 384], F32) as sscr1, \
                 nc.sbuf_tensor("escr0", [128, 384], F32) as escr0, \
                 nc.sbuf_tensor("escr1", [128, 384], F32) as escr1, \
                 nc.sbuf_tensor("pscr0", [128, 384], BF16) as pscr0, \
                 nc.sbuf_tensor("pscr1", [128, 384], BF16) as pscr1, \
                 nc.sbuf_tensor("ptscr0", [128, 384], BF16) as ptscr0, \
                 nc.sbuf_tensor("ptscr1", [128, 384], BF16) as ptscr1, \
                 nc.sbuf_tensor("dscr", [128, 256], F32) as dscr, \
                 nc.psum_tensor("psc0", [128, 384], F32) as psc0, \
                 nc.psum_tensor("psc1", [128, 384], F32) as psc1, \
                 nc.psum_tensor("pt0", [128, 128], BF16) as pt0, \
                 nc.psum_tensor("pt1", [128, 128], BF16) as pt1, \
                 nc.psum_tensor("pt2", [128, 128], BF16) as pt2, \
                 nc.psum_tensor("pc0", [128, 128], F32) as pc0:
                sscr = [sscr0, sscr1]
                escr = [escr0, escr1]
                pscr = [pscr0, pscr1]
                ptscr = [ptscr0, ptscr1]
                psc = [psc0[:], psc1[:]]
                ptp = [pt0[:], pt1[:], pt2[:]]
                pcp = [pc0[:]]
                mm0 = sems['mm'].v
                v0 = sems['v'].v
                sq0 = sems['sq'].v
                tp0 = sems['tp'].v
                ev0 = sems['ev'].v
                NIT = NH * NQB
                with nc.Block() as blk:
                    @blk.tensor
                    def _(t):
                        for it in range(NIT):
                            h, qb = divmod(it, NQB)
                            if it >= 2:
                                t.wait_ge(sems['v'].h, v0 + 4 * (it - 1))
                            t.matmul(psc[it % 2][:, :],
                                     qol[:, h * TOK + qb * 128:
                                         h * TOK + qb * 128 + 128],
                                     k_sb[:, h * EXT + qb * 128:
                                          h * EXT + qb * 128 + 384],
                                     start=True, stop=True) \
                             .then_inc(sems['mm'].h, 1)
                            t.wait_ge(sems['v'].h, v0 + 4 * it + 3)
                            for kb in range(3):
                                t.transpose(ptp[kb][:, :],
                                            pscr[it % 2][:, kb * 128:
                                                         (kb + 1) * 128],
                                            idn[:, :]) \
                                 .then_inc(sems['tp'].h, 1)
                            t.wait_ge(sems['v'].h, v0 + 4 * it + 4)
                            if it >= 1:
                                t.wait_ge(sems['ev'].h, ev0 + it)
                            for kb in range(3):
                                mmi = t.matmul(
                                    pcp[0][:, :],
                                    vT_sb[:, (h * 10 + qb + kb) * 128:
                                          (h * 10 + qb + kb + 1) * 128],
                                    ptscr[it % 2][:, kb * 128:(kb + 1) * 128],
                                    start=(kb == 0), stop=(kb == 2))
                            mmi.then_inc(sems['mm'].h, 1)

                    @blk.vector
                    def _(v):
                        for it in range(NIT):
                            h, qb = divmod(it, NQB)
                            v.wait_ge(sems['mm'].h, mm0 + 2 * it + 1)
                            rb = 4 * (it % 2)
                            if it >= 2:
                                v.wait_ge(sems['sq'].h, sq0 + it - 1)
                            v.tensor_add(sscr[it % 2][:, :],
                                         mask_sb[:, qb * 384:(qb + 1) * 384],
                                         psc[it % 2][:, :])
                            v.reduce_max(redm[:, rb + 1:rb + 2],
                                         sscr[it % 2][:, :], axis=AX.X,
                                         negate=True)
                            v.tensor_copy(dscr[:, :], mask_sb[:, 0:256]) \
                             .then_inc(sems['v'].h, 2)
                            v.wait_ge(sems['sq'].h, sq0 + it + 1)
                            v.tensor_copy(dscr[:, :], mask_sb[:, 0:256])
                            v.reciprocal(redm[:, rb + 2:rb + 3],
                                         redm[:, rb + 3:rb + 4])
                            v.tensor_copy(dscr[:, :], mask_sb[:, 0:256])
                            v.tensor_scalar_mul(pscr[it % 2][:, :],
                                                escr[it % 2][:, :],
                                                redm[:, rb + 2:rb + 3]) \
                             .then_inc(sems['v'].h, 1)
                            v.wait_ge(sems['tp'].h, tp0 + 3 * it + 3)
                            for kb in range(3):
                                cpi = v.tensor_copy(
                                    ptscr[it % 2][:, kb * 128:(kb + 1) * 128],
                                    ptp[kb][:, :])
                            cpi.then_inc(sems['v'].h, 1)
                    sems['v'].v += 4 * NIT

                    @blk.scalar
                    def _(s):
                        for it in range(NIT):
                            s.wait_ge(sems['v'].h, v0 + 4 * it + 2)
                            rb = 4 * (it % 2)
                            s.activation(escr[it % 2][:, :],
                                         sscr[it % 2][:, :],
                                         AF.Exp, bias=redm[:, rb + 1:rb + 2],
                                         accum_out=redm[:, rb + 3:rb + 4]) \
                             .then_inc(sems['sq'].h, 1)
                            if it >= 1:
                                s.wait_ge(sems['mm'].h, mm0 + 2 * it)
                                h1_, qb1 = divmod(it - 1, NQB)
                                s.activation(
                                    ctxT[:, h1_ * TOK + qb1 * 128:
                                         h1_ * TOK + qb1 * 128 + 128],
                                    pcp[0][:, :], AF.Copy) \
                                 .then_inc(sems['ev'].h, 1)
                        s.wait_ge(sems['mm'].h, mm0 + 2 * NIT)
                        h1_, qb1 = divmod(NIT - 1, NQB)
                        s.activation(ctxT[:, h1_ * TOK + qb1 * 128:
                                          h1_ * TOK + qb1 * 128 + 128],
                                     pcp[0][:, :], AF.Copy) \
                         .then_inc(sems['ev'].h, 1)
                    sems['sq'].v += NIT
                    sems['ev'].v += NIT
                    sems['mm'].v += 2 * NIT
                    sems['tp'].v += 3 * NIT

            if ATTN_DEBUG:
                dbg_ctx = bf16ap(nc, "dbg_ctx", [D, TOK],
                                 kind="ExternalOutput")
                with nc.Block() as blk:
                    @blk.sync
                    def _(e):
                        e.dma_start(
                            out=dbg_ctx.rearrange("(g p) t -> p g t", p=128),
                            in_=ctxT[:].rearrange("p (g t) -> p g t", t=TOK)) \
                         .then_inc(sems['ldx'].h, 16)
                        e.wait_ge(sems['ldx'].h, sems['ldx'].v + 16)
                sems['ldx'].v += 16

            def evac_ol(s, dg, ci, ps):
                return s.activation(
                    qol[:, dg * TOK + ci * 512:dg * TOK + (ci + 1) * 512],
                    ps, AF.Copy)
            proj_phase(wot, ctxT[:], 0, TOK, [(0, 512), (512, 512)], evac_ol)

        if ATTN_DEBUG:
            dbg_ol = bf16ap(nc, "dbg_ol", [D, TOK], kind="ExternalOutput")
            with nc.Block() as blk:
                @blk.sync
                def _(e):
                    e.dma_start(
                        out=dbg_ol.rearrange("(g p) t -> p g t", p=128),
                        in_=qol[:].rearrange("p (g t) -> p g t", t=TOK)) \
                     .then_inc(sems['ldx'].h, 16)
                    e.wait_ge(sems['ldx'].h, sems['ldx'].v + 16)
            sems['ldx'].v += 16

        es_kv.close()

        es_sim = ExitStack()
        sim_sb = es_sim.enter_context(
            nc.sbuf_tensor("sim_sb", [128, NQB * M], BF16))
        with nc.sbuf_tensor("xk_sb", [128, G * TOK], BF16) as xk_sb, \
             nc.sbuf_tensor("mtb0", [128, G * 512], BF16) as mtb0, \
             nc.sbuf_tensor("mtb1", [128, G * 512], BF16) as mtb1:
            mtbs = [mtb0, mtb1]
            mm0 = sems['mm'].v
            ev0 = sems['ev'].v
            mld = PPStream(pp_g)
            xk_done = [(pp_s2.s[i].h, stx.base[i] + 16 * 8)
                       for i in range(2)]
            with nc.Block() as blk:
                @blk.sync
                def _(e):
                    for hs_, hv_ in xk_done:
                        e.wait_ge(hs_, hv_)
                    e.dma_start(
                        out=xk_sb[:].rearrange("p (g t) -> p g t", t=TOK),
                        in_=xk_hbm.rearrange("(g p) t -> p g t", p=128)) \
                     .then_inc(sems['ldx'].h, 16)
                sems['ldx'].v += 16
                ldx_all = sems['ldx'].v

                @blk.gpsimd
                def _(g2):
                    for ms in range(8):
                        if ms >= 2:
                            g2.wait_ge(sems['mm'].h, mm0 + 8 * (ms - 1))
                        g2.dma_start(
                            out=mtbs[ms % 2][:].rearrange(
                                "p (kg m) -> p kg m", m=512),
                            in_=membt.rearrange("(kg p) m -> p kg m",
                                                p=128)[:, :,
                                                       ms * 512:
                                                       (ms + 1) * 512]) \
                          .then_inc(mld.inc(ms), 16)

                @blk.tensor
                def _(t):
                    t.wait_ge(sems['ldx'].h, ldx_all)
                    for ms in range(8):
                        mld.wait(t, ms)
                        for tb in range(NQB):
                            gi = 8 * ms + tb
                            if gi >= 2:
                                t.wait_ge(sems['ev'].h, ev0 + gi - 1)
                            for kg in range(G):
                                mmi = t.matmul(
                                    psums[gi % 2],
                                    xk_sb[:, kg * TOK + tb * 128:
                                          kg * TOK + tb * 128 + 128],
                                    mtbs[ms % 2][:, kg * 512:(kg + 1) * 512],
                                    start=(kg == 0), stop=(kg == G - 1))
                            mmi.then_inc(sems['mm'].h, 1)
                sems['mm'].v += 64

                @blk.scalar
                def _(s):
                    for ms in range(8):
                        for tb in range(NQB):
                            gi = 8 * ms + tb
                            s.wait_ge(sems['mm'].h, mm0 + gi + 1)
                            s.activation(
                                sim_sb[:, tb * M + ms * 512:
                                       tb * M + (ms + 1) * 512],
                                psums[gi % 2], AF.Copy) \
                             .then_inc(sems['ev'].h, 1)
                sems['ev'].v += 64

        if ATTN_DEBUG:
            dbg_sim = bf16ap(nc, "dbg_sim", [128, NQB * M],
                             kind="ExternalOutput")
            with nc.Block() as blk:
                @blk.sync
                def _(e):
                    e.dma_start(out=dbg_sim, in_=sim_sb[:, :]) \
                     .then_inc(sems['ldx'].h, 16)
                    e.wait_ge(sems['ldx'].h, sems['ldx'].v + 16)
            sems['ldx'].v += 16

        with nc.sbuf_tensor("et0", [128, M], F32) as et0, \
             nc.sbuf_tensor("et1", [128, M], F32) as et1, \
             nc.sbuf_tensor("wf0", [128, M], F32) as wf0, \
             nc.sbuf_tensor("mtl", [128, M], F32) as mtl, \
             nc.sbuf_tensor("m8a", [128, 8 * 8], BF16) as m8a, \
             nc.sbuf_tensor("m8f", [128, 8 * 8], F32) as m8f, \
             nc.sbuf_tensor("red2", [128, 32], F32) as red2:
            ets = [et0, et1]
            wfs = [wf0, wf0]
            v0 = sems['v'].v
            ev0 = sems['ev'].v
            sq0 = sems['sq'].v
            with nc.Block() as blk:
                @blk.vector
                def _(v):
                    for tb in range(NQB):
                        v.max(out=m8a[:, tb * 8:(tb + 1) * 8],
                              in_=sim_sb[:, tb * M:(tb + 1) * M])
                    for tb in range(NQB):
                        v.reduce_max(red2[:, tb:tb + 1],
                                     sim_sb[:, tb * M:(tb + 1) * M],
                                     axis=AX.X)
                    for tb in range(NQB):
                        v.tensor_copy(m8f[:, tb * 8:(tb + 1) * 8],
                                      m8a[:, tb * 8:(tb + 1) * 8])
                    for tb in range(NQB):
                        sl = slice(tb * M, (tb + 1) * M)
                        if tb == 1:
                            v.tensor_copy(mtl[:, 0:1024],
                                          sim_sb[:, M:M + 1024])
                        if tb >= 1:
                            v.reciprocal(red2[:, 24 + tb - 1:25 + tb - 1],
                                         red2[:, 16 + tb - 1:17 + tb - 1])
                        v.tensor_scalar(ets[tb % 2][:, :], sim_sb[:, sl],
                                        red2[:, tb:tb + 1], None,
                                        op0=ALU.subtract) \
                         .then_inc(sems['v'].h, 1)
                        v.tensor_scalar(mtl[:, :], sim_sb[:, sl],
                                        m8f[:, tb * 8 + 7:tb * 8 + 8], None,
                                        op0=ALU.is_ge)
                        v.wait_ge(sems['sq'].h, sq0 + tb + 1)
                        v.tensor_mul(ets[tb % 2][:, :], wfs[tb % 2][:, :],
                                     mtl[:, :]).then_inc(sems['ev'].h, 1)
                        v.reduce_sum(red2[:, 16 + tb:17 + tb],
                                     ets[tb % 2][:, :], axis=AX.X)
                        if tb >= 1:
                            sl1 = slice((tb - 1) * M, tb * M)
                            v.tensor_scalar(
                                sim_sb[:, sl1], ets[(tb - 1) % 2][:, :],
                                red2[:, 24 + tb - 1:25 + tb - 1], None,
                                op0=ALU.mult)
                    v.reciprocal(red2[:, 31:32], red2[:, 23:24])
                    v.tensor_copy(mtl[:, 0:1024], sim_sb[:, 0:1024])
                    v.tensor_scalar(sim_sb[:, 7 * M:8 * M],
                                    ets[7 % 2][:, :],
                                    red2[:, 31:32], None, op0=ALU.mult)
                    v.tensor_copy(mtl[:, :], ets[1][:, :])
                    v.tensor_copy(ets[0][:, :], mtl[:, :]) \
                     .then_inc(sems['v'].h, 1)
                sems['v'].v += NQB + 1
                settle_m2 = sems['v'].v

                @blk.scalar
                def _(s):
                    for tb in range(NQB):
                        s.wait_ge(sems['v'].h, v0 + tb + 1)
                        if tb >= 1:
                            s.wait_ge(sems['ev'].h, ev0 + tb)
                        s.activation(wfs[tb % 2][:, :], ets[tb % 2][:, :],
                                     AF.Exp).then_inc(sems['sq'].h, 1)
                sems['sq'].v += NQB
                sems['ev'].v += NQB

        if ATTN_DEBUG:
            dbg_wsp = bf16ap(nc, "dbg_wsp", [128, NQB * M],
                             kind="ExternalOutput")
            with nc.Block() as blk:
                @blk.sync
                def _(e):
                    e.dma_start(out=dbg_wsp, in_=sim_sb[:, :]) \
                     .then_inc(sems['ldx'].h, 16)
                    e.wait_ge(sems['ldx'].h, sems['ldx'].v + 16)
            sems['ldx'].v += 16

        es_oacc = ExitStack()
        oacc = es_oacc.enter_context(
            nc.sbuf_tensor("oacc", [128, G * TOK], BF16))
        with nc.sbuf_tensor("msl", [128, 4 * D], BF16) as msl, \
             nc.sbuf_tensor("wspT", [128, 4 * TOK], BF16) as wspT, \
             nc.psum_tensor("pstA2", [128, 128], BF16) as pstA2, \
             nc.psum_tensor("pstB2", [128, 128], BF16) as pstB2:
            pst = [pstA2[:], pstB2[:]]
            mm0 = sems['mm'].v
            tp0 = sems['tp'].v
            v0 = sems['v'].v
            y30 = sems['y3'].v
            mld = PPStream(pp_g)
            NQI = 8
            with nc.Block() as blk:
                @blk.gpsimd
                def _(g2):
                    for qi in range(NQI):
                        for j in range(4):
                            i = 4 * qi + j
                            if qi >= 1:
                                g2.wait_ge(sems['mm'].h, mm0 + 32 * qi)
                            g2.dma_start(
                                out=msl[:, j * D:(j + 1) * D],
                                in_=memb.rearrange("(r p) m -> r p m",
                                                   p=128)[4 * qi + j]) \
                              .then_inc(mld.inc(i), 16)

                @blk.tensor
                def _(t):
                    t.wait_ge(sems['v'].h, settle_m2)
                    ntp = 0
                    for qi in range(NQI):
                        if qi >= 2:
                            t.wait_ge(sems['y3'].h, y30 + 32 * (qi - 1))
                        for j in range(4):
                            for tb in range(NQB):
                                if ntp >= 2:
                                    t.wait_ge(sems['v'].h, v0 + ntp - 1)
                                t.transpose(
                                    pst[ntp % 2][:, :],
                                    sim_sb[:, tb * M + (4 * qi + j) * 128:
                                           tb * M + (4 * qi + j + 1) * 128],
                                    idn[:, :]).then_inc(sems['tp'].h, 1)
                                ntp += 1
                        for j in range(4):
                            mld.wait(t, 4 * qi + j)
                        t.wait_ge(sems['v'].h, v0 + 32 * (qi + 1))
                        for dg in range(G):
                            for nb in range(2):
                                gi = 2 * dg + nb
                                if qi >= 1 or gi >= 2:
                                    t.wait_ge(sems['y3'].h,
                                              y30 + 32 * qi + gi - 1)
                                for j in range(4):
                                    mmi = t.matmul(
                                        psums[gi % 2],
                                        msl[:, j * D + dg * 128:
                                            j * D + (dg + 1) * 128],
                                        wspT[:, j * TOK + nb * 512:
                                             j * TOK + (nb + 1) * 512],
                                        start=(j == 0), stop=(j == 3))
                                mmi.then_inc(sems['mm'].h, 1)
                sems['mm'].v += 32 * NQI
                sems['tp'].v += 32 * NQI

                @blk.vector
                def _(v):
                    cnt = 0
                    for qi in range(NQI):
                        if qi >= 1:
                            v.wait_ge(sems['mm'].h, mm0 + 32 * qi)
                        for j in range(4):
                            for tb in range(NQB):
                                cnt += 1
                                v.wait_ge(sems['tp'].h, tp0 + cnt)
                                v.tensor_copy(
                                    wspT[:, j * TOK + tb * 128:
                                         j * TOK + tb * 128 + 128],
                                    pst[(cnt - 1) % 2][:, :]) \
                                 .then_inc(sems['v'].h, 1)
                        for dg in range(G):
                            for nb in range(2):
                                gi = 2 * dg + nb
                                v.wait_ge(sems['mm'].h,
                                          mm0 + 32 * qi + gi + 1)
                                sl = slice(dg * TOK + nb * 512,
                                           dg * TOK + (nb + 1) * 512)
                                if qi == 0:
                                    vi = v.tensor_copy(oacc[:, sl],
                                                       psums[gi % 2])
                                else:
                                    vi = v.tensor_add(oacc[:, sl],
                                                      oacc[:, sl],
                                                      psums[gi % 2])
                                vi.then_inc(sems['y3'].h, 1)
                sems['v'].v += 32 * NQI
                sems['y3'].v += 32 * NQI

        if ATTN_DEBUG:
            dbg_oacc = bf16ap(nc, "dbg_oacc", [D, TOK],
                              kind="ExternalOutput")
            with nc.Block() as blk:
                @blk.sync
                def _(e):
                    e.dma_start(
                        out=dbg_oacc.rearrange("(g p) t -> p g t", p=128),
                        in_=oacc[:].rearrange("p (g t) -> p g t", t=TOK)) \
                     .then_inc(sems['ldx'].h, 16)
                    e.wait_ge(sems['ldx'].h, sems['ldx'].v + 16)
            sems['ldx'].v += 16

        with nc.sbuf_tensor("ovT", [128, G * TOK], BF16) as ovT, \
             nc.sbuf_tensor("omt0", [128, TOK], BF16) as omt0, \
             nc.sbuf_tensor("omt1", [128, TOK], BF16) as omt1:
            omts = [omt0, omt1]

            def evac_ov(s, dg, ci, ps):
                return s.activation(
                    ovT[:, dg * TOK + ci * 512:dg * TOK + (ci + 1) * 512],
                    ps, AF.Copy)
            proj_phase(wvt, oacc[:], 0, TOK, [(0, 512), (512, 512)], evac_ov)

            stm_om = PPStream(pp_s2)
            ev_b2 = sems['ev'].v

            def evac_om(s, dg, ci, ps):
                if dg >= 2 and ci == 0:
                    s_ = pp_s2.s[dg % 2]
                    s.wait_ge(s_.h, stm_om.base[dg % 2] +
                              16 * ((dg - 2) // 2 + 1))
                return s.activation(
                    omts[dg % 2][:, ci * 512:(ci + 1) * 512], ps, AF.Copy)

            def extra_om(blk):
                @blk.sync
                def _(e):
                    for dg in range(G):
                        e.wait_ge(sems['ev'].h, ev_b2 + 2 * dg + 2)
                        e.dma_start(
                            out=om_hbm.rearrange("(g p) t -> g p t",
                                                 p=128)[dg],
                            in_=omts[dg % 2][:, :]) \
                          .then_inc(stm_om.inc(dg), 16)

            proj_phase(wot, ovT[:], 0, TOK, [(0, 512), (512, 512)], evac_om,
                       extra_emit=extra_om)
            om_done = [(pp_s2.s[i].h, pp_s2.s[i].v) for i in range(2)]

        es_oacc.close()
        es_sim.close()

        with nc.sbuf_tensor("gw_sb", [128, G * 128], BF16) as gw_sb, \
             nc.sbuf_tensor("pgs_sb", [128, TOK], F32) as pgs_sb, \
             nc.sbuf_tensor("xmc0", [128, TOK], F32) as xmc0, \
             nc.sbuf_tensor("xmc1", [128, TOK], F32) as xmc1, \
             nc.sbuf_tensor("omc0", [128, TOK], BF16) as omc0, \
             nc.sbuf_tensor("omc1", [128, TOK], BF16) as omc1, \
             nc.sbuf_tensor("tsc", [128, TOK], F32) as tsc, \
             nc.sbuf_tensor("xvc0", [128, TOK], F32) as xvc0, \
             nc.sbuf_tensor("xvc1", [128, TOK], F32) as xvc1, \
             nc.psum_tensor("pgb0", [128, 512], F32) as pgb0, \
             nc.psum_tensor("pgb1", [128, 512], F32) as pgb1:
            xmcs = [xmc0, xmc1]
            omcs = [omc0, omc1]
            xvcs = [xvc0, xvc1]
            pgb = [pgb0[:], pgb1[:]]
            mm0 = sems['mm'].v
            sq0 = sems['sq'].v
            ldg0 = sems['ldg'].v
            with nc.Block() as blk:
                @blk.gpsimd
                def _(g2):
                    g2.dma_start(
                        out=gw_sb[:].rearrange("p (kg m) -> p kg m", m=128),
                        in_=gw.rearrange("(kg p) m -> p kg m", p=128)) \
                      .then_inc(sems['ldg'].h, 16)

                @blk.tensor
                def _(t):
                    t.wait_ge(sems['ldg'].h, ldg0 + 16)
                    for nb in range(2):
                        for kg in range(G):
                            mmi = t.matmul(
                                pgb[nb],
                                gw_sb[:, kg * 128:(kg + 1) * 128],
                                qol[:, kg * TOK + nb * 512:
                                    kg * TOK + (nb + 1) * 512],
                                start=(kg == 0), stop=(kg == G - 1))
                        mmi.then_inc(sems['mm'].h, 1)
                sems['mm'].v += 2
                sems['ldg'].v += 16

                @blk.scalar
                def _(s):
                    for nb in range(2):
                        s.wait_ge(sems['mm'].h, mm0 + nb + 1)
                        s.activation(pgs_sb[:, nb * 512:(nb + 1) * 512],
                                     pgb[nb], AF.Sigmoid,
                                     bias=gb_sb[:, 0:1]) \
                         .then_inc(sems['sq'].h, 1)
                sems['sq'].v += 2
            v0 = sems['v'].v
            ld = PPStream(pp_s)
            ldo = PPStream(pp_g)
            stm = PPStream(pp_s2)
            xm3 = xin.rearrange("(g p) t -> g p t", p=128)
            xo3 = xo.rearrange("(g p) t -> g p t", p=128)
            om3 = om_hbm.rearrange("(g p) t -> g p t", p=128)
            with nc.Block() as blk:
                @blk.sync
                def _(e):
                    for dg in range(G):
                        if dg >= 2:
                            e.wait_ge(sems['v'].h, v0 + 3 * (dg - 1))
                        e.dma_start(out=xmcs[dg % 2][:, :],
                                    in_=xm3[dg, :, :]) \
                         .then_inc(ld.inc(dg), 16)
                        if dg >= 1:
                            e.wait_ge(sems['v'].h, v0 + 3 * dg)
                            e.dma_start(out=xo3[dg - 1, :, :],
                                        in_=xvcs[(dg - 1) % 2][:, :]) \
                             .then_inc(stm.inc(dg - 1), 16)
                    e.wait_ge(sems['v'].h, v0 + 3 * G)
                    e.dma_start(out=xo3[G - 1, :, :],
                                in_=xvcs[(G - 1) % 2][:, :]) \
                     .then_inc(stm.inc(G - 1), 16)

                @blk.gpsimd
                def _(g2):
                    for hs_, hv_ in om_done:
                        g2.wait_ge(hs_, hv_)
                    for dg in range(G):
                        if dg >= 2:
                            g2.wait_ge(sems['v'].h, v0 + 3 * (dg - 1))
                        g2.dma_start(out=omcs[dg % 2][:, :],
                                     in_=om3[dg, :, :]) \
                          .then_inc(ldo.inc(dg), 16)

                @blk.vector
                def _(v):
                    for dg in range(G):
                        ld.wait(v, dg)
                        ldo.wait(v, dg)
                        if dg >= 2:
                            stm.wait(v, dg - 2)
                        if dg == 0:
                            v.wait_ge(sems['sq'].h, sq0 + 2)
                        for nb in range(2):
                            slf = slice(nb * 512, (nb + 1) * 512)
                            v.tensor_mul(tsc[:, slf], omcs[dg % 2][:, slf],
                                         pgs_sb[:, slf])
                        v.tensor_add(tsc[:, :], tsc[:, :],
                                     qol[:, dg * TOK:(dg + 1) * TOK])
                        v.tensor_add(xvcs[dg % 2][:, :], tsc[:, :],
                                     xmcs[dg % 2][:, :]) \
                         .then_inc(sems['v'].h, 3)
                sems['v'].v += 3 * G

        es_q.close()
        es_ps.close()

        xo_done = [(pp_s2.s[i].h, pp_s2.s[i].v) for i in range(2)]
        emit_ln_stream(nc, sems, pp_s, pp_o, xo, uo, TOK, ones_c[:, :],
                       ones_r[:, :], stat_row, wait_sems=xo_done)

        with nc.Block() as blk:
            final = [(pp_o.s[i].h, pp_o.s[i].v) for i in range(2)]

            @blk.sync
            def _(e):
                for hs_, hv_ in final:
                    e.wait_ge(hs_, hv_)
    return nc


def prep_attn_weights(inputs):
    c = {}
    c["wqt"] = np.ascontiguousarray(inputs["L1_attn_wq"].T)
    c["wkt"] = np.ascontiguousarray(inputs["L1_attn_wk"].T)
    c["wvt"] = np.ascontiguousarray(inputs["L1_attn_wv"].T)
    c["wot"] = np.ascontiguousarray(inputs["L1_attn_wo"].T)
    c["wkn"] = np.ascontiguousarray(inputs["L1_attn_wk"])
    c["membt"] = np.ascontiguousarray(inputs["L1_attn_mem_bank"].T)
    c["memb"] = np.ascontiguousarray(inputs["L1_attn_mem_bank"])
    c["gw"] = np.ascontiguousarray(
        np.repeat(inputs["L1_attn_gate_w"].reshape(2048, 1), 128, axis=1))
    c["gb"] = np.full((128, 1), float(inputs["L1_attn_gate_b"][0]),
                      np.float32)
    c["ident"] = np.eye(128, dtype=np.float32)
    return c


def attn_masks(half):
    mk = np.zeros((8, 128, 384), np.float32)
    for qb in range(8):
        for kb in range(3):
            i = np.arange(128)[:, None]
            j = np.arange(128)[None, :]
            delta = 256 - kb * 128 + i - j
            gk = half * 1024 - 256 + qb * 128 + kb * 128 + j
            ok = (delta >= 0) & (delta <= 255) & (gk >= 0)
            mk[qb, :, kb * 128:(kb + 1) * 128] = np.where(ok, 0.0, -1e30)
    return np.ascontiguousarray(
        mk.transpose(1, 0, 2).reshape(128, 8 * 384))


def run_attn(x_shards, uext_shards, wc):
    if "attn" not in _cache:
        _cache["attn"] = build_attn()
    nc = _cache["attn"]
    in_maps = []
    for ci in range(NCORE):
        b, half = divmod(ci, 2)
        m = dict(wc)
        m["x"] = x_shards[ci]
        m["uext"] = uext_shards[ci]
        m["masks"] = attn_masks(half)
        in_maps.append(m)
    res = run_bass_kernel_spmd(nc, in_maps, core_ids=list(range(NCORE)))
    xo = [np.asarray(r["xo"]) for r in res.results]
    uo = [np.asarray(r["uo"]) for r in res.results]
    return xo, uo



_cache = {}


def prep_ffn0_weights(inputs):
    c = {}
    c["w1t"] = np.ascontiguousarray(inputs["L0_ffn_W1"].T)
    c["w2t"] = np.ascontiguousarray(inputs["L0_ffn_W2"].T)
    c["w3t"] = np.ascontiguousarray(inputs["L0_ffn_W3"].T)
    c["b1r"] = np.ascontiguousarray(inputs["L0_ffn_b1"].reshape(32, 128).T)
    c["b2r"] = np.ascontiguousarray(inputs["L0_ffn_b2"].reshape(32, 128).T)
    c["b3r"] = np.ascontiguousarray(inputs["L0_ffn_b3"].reshape(16, 128).T)
    return c


def run_ffn0(x_shards, add_shards, wc):
    if "ffn0" not in _cache:
        _cache["ffn0"] = build_ffn0()
    nc = _cache["ffn0"]
    in_maps = []
    for ci in range(NCORE):
        m = dict(wc)
        m["x"] = x_shards[ci]
        m["add"] = add_shards[ci]
        in_maps.append(m)
    res = run_bass_kernel_spmd(nc, in_maps, core_ids=list(range(NCORE)))
    xo = [np.asarray(r["xo"]) for r in res.results]
    uo = [np.asarray(r["uo"]) for r in res.results]
    return xo, uo


def kernel(**inputs):
    inputs = {k: np.asarray(v) for k, v in inputs.items()}
    x0 = inputs["L0_attn_u"]
    wc0 = prep_ffn0_weights(inputs)
    scan_wc = prep_scan_weights(inputs)
    wc1 = prep_ffn1_weights(inputs)
    wca = prep_attn_weights(inputs)

    x_sh = []
    for ci in range(NCORE):
        b, half = divmod(ci, 2)
        x_sh.append(np.ascontiguousarray(
            x0[b, half * TOK:(half + 1) * TOK].T))
    zeros_add = [np.zeros((D, TOK), dtype=BF) for _ in range(NCORE)]

    xo1, uo1 = run_ffn0(x_sh, zeros_add, wc0)
    u_full = [np.ascontiguousarray(np.concatenate(
        [uo1[2 * b], uo1[2 * b + 1]], axis=1)) for b in range(4)]
    ys = run_scan(u_full, scan_wc)
    adds = []
    for ci in range(NCORE):
        b, half = divmod(ci, 2)
        hs = slice(half * TOK, (half + 1) * TOK)
        adds.append(np.ascontiguousarray(
            np.concatenate([ys[b][0][:, hs], ys[b][1][:, hs]], axis=0)))
    xo3, uo3 = run_ffn0(xo1, adds, wc0)
    xo4, uo4 = run_ffn1(xo3, uo3, wc1)
    uexts = []
    for ci in range(NCORE):
        b, half = divmod(ci, 2)
        if half == 0:
            halo = np.zeros((D, 256), dtype=BF)
        else:
            halo = uo4[2 * b][:, TOK - 256:TOK]
        uexts.append(np.ascontiguousarray(
            np.concatenate([halo, uo4[ci]], axis=1)))
    xo5, uo5 = run_attn(xo4, uexts, wca)
    xo6, _ = run_ffn1(xo5, uo5, wc1)

    out = np.empty((4, SEQ, D), dtype=np.float32)
    for ci in range(NCORE):
        b, half = divmod(ci, 2)
        out[b, half * TOK:(half + 1) * TOK] = np.asarray(xo6[ci]).T
    return out
